# version 14
# baseline (speedup 1.0000x reference)
import numpy as np
from contextlib import ExitStack

import concourse.bass as bass
import concourse.mybir as mybir
from concourse.bass_utils import run_bass_kernel_spmd

F32 = mybir.dt.float32
BF16 = mybir.dt.bfloat16
AF = mybir.ActivationFunctionType
Alu = mybir.AluOpType
AX = mybir.AxisListType.X

MASK_NEG = -1.0e9
THR_BF16 = 0.501953125
POOL_QUANT = False

C_INV64, C_P125, C_EPS_LN, C_EPS_RMS, C_INVHF, C_EPSF32, C_NEG1 = range(7)
NCONST = 8


class Cfg:
    def __init__(self, S=4096, D=2048, NG=8, HPG=4, HF=None, debug=0):
        self.S, self.D, self.NG, self.HPG = S, D, NG, HPG
        self.debug = debug
        self.DH = 64
        self.NH = NG * HPG
        self.HF = HF if HF is not None else 4 * D
        self.FC = D // 128
        self.QT = S // 32
        self.SEG = S // 8
        self.NSLOT = 8
        self.OWN = self.NSLOT * self.QT
        self.QCOLS = self.NH * self.DH
        self.KCOLS = NG * self.DH
        self.CC = self.QCOLS // 128
        self.NPAIR = NG // 2
        self.NT = S // 128
        self.OT = self.OWN // 128
        self.HCH = self.HF // 128


_CHAIN_OPS = ("tensor_reduce", "tensor_scalar", "tensor_tensor", "tensor_copy",
              "reciprocal", "memset", "scalar_tensor_tensor", "activation",
              "tensor_add", "tensor_sub", "tensor_mul")


class _AlreadyChained:
    pass


class _Chained:

    def __init__(self, em, eng, sem_name):
        self._em, self._eng, self._s = em, eng, sem_name

    def __getattr__(self, name):
        f = getattr(self._eng, name)
        if name not in _CHAIN_OPS:
            return f

        def g(*a, **k):
            ins = f(*a, **k)
            self._em.inc(ins, self._s, 1)
            self._eng.drain()
            return _AlreadyChained()
        return g


class B:

    def __init__(self, cfg: Cfg):
        self.cfg = cfg
        self.nc = bass.Bass("TRN2", target_bir_lowering=False)
        self.nc.detect_race_conditions = False
        self.es = ExitStack()
        self.c = {}
        self.sem = {}
        for s in ("sL", "sH", "sT", "sA", "sV", "sG"):
            self.sem[s] = self.es.enter_context(self.nc.semaphore(s))
            self.c[s] = 0

    def inc(self, instr, name, amt):
        if isinstance(instr, _AlreadyChained):
            return instr
        self.c[name] += amt
        return instr.then_inc(self.sem[name], amt)

    def snap(self):
        return dict(self.c)

    def wge(self, eng, *snaps):
        if getattr(self, "serial", False):
            return self.w(eng)
        m = {}
        for s in snaps:
            if not s:
                continue
            for k, v in s.items():
                if v > m.get(k, 0):
                    m[k] = v
        for k, v in m.items():
            if v > 0:
                eng.wait_ge(self.sem[k], v)

    def w(self, eng, *names):
        for n in self.c:
            if self.c[n] > 0:
                eng.wait_ge(self.sem[n], self.c[n])

    def sb(self, st, name, shape, dt, side=None):
        if side is None:
            return st.enter_context(self.nc.sbuf_tensor(name, shape, dt))
        return st.enter_context(self.nc.sbuf_tensor(name, shape, dt, side=side))

    def ps(self, st, name, shape, dt=F32):
        return st.enter_context(self.nc.psum_tensor(name, shape, dt))

    def build(self):
        cfg, nc = self.cfg, self.nc
        S, D, NG, HPG, DH = cfg.S, cfg.D, cfg.NG, cfg.HPG, cfg.DH
        HF, FC, QT, SEG = cfg.HF, cfg.FC, cfg.QT, cfg.SEG
        QCOLS, KCOLS, CC, NPAIR = cfg.QCOLS, cfg.KCOLS, cfg.CC, cfg.NPAIR
        OWN, NT, OT, NSLOT = cfg.OWN, cfg.NT, cfg.OT, cfg.NSLOT
        SEGB = SEG // 128

        dt_ = nc.dram_tensor
        x_seq = dt_("x_seq", [S, D], F32, kind="ExternalInput")
        x_own = dt_("x_own", [OWN, D], F32, kind="ExternalInput")
        w_in = dt_("w_in", [QCOLS + 2 * KCOLS, D], F32, kind="ExternalInput")
        w_out = dt_("w_out", [D, QCOLS], F32, kind="ExternalInput")
        w_gu = dt_("w_gu", [2 * HF, D], F32, kind="ExternalInput")
        w_proj = dt_("w_proj", [D, HF], F32, kind="ExternalInput")
        rms_rep = dt_("rms_rep", [128, D], F32, kind="ExternalInput")
        kcos_d = dt_("kcos", [S, 32], F32, kind="ExternalInput")
        ksin_d = dt_("ksin", [S, 32], F32, kind="ExternalInput")
        qcos_d = dt_("qcos", [OWN, 32], F32, kind="ExternalInput")
        qsin_d = dt_("qsin", [OWN, 32], F32, kind="ExternalInput")
        tmaskb_d = dt_("trimaskb", [QT, SEG], BF16, kind="ExternalInput")
        identb_d = dt_("identb", [128, 128], BF16, kind="ExternalInput")
        consts_d = dt_("consts", [128, NCONST], F32, kind="ExternalInput")
        ones_d = dt_("ones_in", [128, 64], F32, kind="ExternalInput")
        ident_d = dt_("ident_in", [128, 128], F32, kind="ExternalInput")
        onesb_d = dt_("onesb_in", [128, 1], BF16, kind="ExternalInput")
        y_spill = dt_("y_spill", [OWN, D], F32)
        out_own = dt_("out_own", [OWN, D], F32, kind="ExternalOutput")
        if cfg.debug:
            dbg_K2 = dt_("dbg_K2", [128, NPAIR * S], F32, kind="ExternalOutput")
            dbg_Vt = dt_("dbg_Vt", [128, NG * (S // 128) * (DH + 1)], F32, kind="ExternalOutput")
            dbg_Qp = dt_("dbg_Qp", [128, NPAIR * HPG * OWN], F32, kind="ExternalOutput")
            dbg_ofm = dt_("dbg_ofm", [128, CC * OWN], F32, kind="ExternalOutput")
            dbg_xnf = dt_("dbg_xnf", [128, FC * OWN], F32, kind="ExternalOutput")
            dbg_gacc = dt_("dbg_gacc", [128, OT * D], F32, kind="ExternalOutput")
            dbg_muI = dt_("dbg_muI", [128, max(OT, 8)], F32, kind="ExternalOutput")
            dbg_psT = dt_("dbg_psT", [128, 512], F32, kind="ExternalOutput")
            dbg_st = dt_("dbg_st", [128, 8 + NCONST], F32, kind="ExternalOutput")
            dbg_xrb = dt_("dbg_xrb", [128, D], F32, kind="ExternalOutput")
            dbg_kcos = dt_("dbg_kcos", [128, NT * 32], F32, kind="ExternalOutput")
            dbg_wkvT = dt_("dbg_wkvT", [128, FC * 2 * KCOLS], F32, kind="ExternalOutput")
            dbg_sqh = dt_("dbg_sqh", [128, min(512, OWN)], F32, kind="ExternalOutput")
            dbg_s1 = dt_("dbg_s1", [QT, 2 * HPG * 8], F32, kind="ExternalOutput")
            dbg_be = dt_("dbg_be", [QT, 2 * HPG], F32, kind="ExternalOutput")
            dbg_Pb0 = dt_("dbg_Pb0", [128, SEG], F32, kind="ExternalOutput")
            dbg_s2 = dt_("dbg_s2", [QT, 2 * HPG * 8], F32, kind="ExternalOutput")
            dbg_rr = dt_("dbg_rr", [QT, 2 * HPG], F32, kind="ExternalOutput")
            dbg_ost = dt_("dbg_ost", [128, HPG * QT], F32, kind="ExternalOutput")
            dbg_bc = dt_("dbg_bc", [128, HPG * QT], F32, kind="ExternalOutput")

        es = self.es
        consts = self.sb(es, "consts_sb", [128, NCONST], F32)
        ones_f = self.sb(es, "ones_f", [128, 64], F32)
        ident_f = self.sb(es, "ident_f", [128, 128], F32)
        ones_b = self.sb(es, "ones_b", [128, 1], BF16)
        tmaskb = self.sb(es, "tmaskb_sb", [QT, SEG], BF16)
        identb = self.sb(es, "identb_sb", [128, 128], BF16)
        muR = self.sb(es, "muR", [128, OT * 2], F32)

        patt = ExitStack()
        K2 = self.sb(patt, "K2", [128, NPAIR * S], BF16)
        Vt = self.sb(patt, "Vt", [128, NG * (S // 128) * (DH + 1)], BF16)
        pQp = ExitStack()

        VST = (S // 128) * (DH + 1)

        blk = es.enter_context(nc.Block())

        def cAP(col, p=128):
            return consts[0:p, col:col + 1]

        W = self

        def preload(func):
            @blk.scalar
            def _(sc):
                sc.preload_act_table(func)

        def phase_barrier():
            for eng in (blk.tensor, blk.vector, blk.scalar, blk.gpsimd, blk.sync):
                @eng
                def _(e):
                    W.w(e)

        p0 = ExitStack()
        kcos = self.sb(p0, "kcos_sb", [128, NT * 32], F32, side="right")
        ksin = self.sb(p0, "ksin_sb", [128, NT * 32], F32, side="right")
        qcos = self.sb(p0, "qcos_sb", [128, OT * 32], F32, side="right")
        qsin = self.sb(p0, "qsin_sb", [128, OT * 32], F32, side="right")
        rmsw = self.sb(p0, "rmsw_sb", [128, D], F32, side="right")


        @blk.sync
        def _(sy):
            for dst, src in ((consts, consts_d), (ones_f, ones_d), (ident_f, ident_d),
                             (tmaskb, tmaskb_d), (identb, identb_d),
                             (rmsw, rms_rep), (ones_b, onesb_d)):
                W.inc(sy.dma_start(dst.ap(), src.ap()), "sH", 16)
            for dst, src in ((kcos, kcos_d), (ksin, ksin_d), (qcos, qcos_d), (qsin, qsin_d)):
                W.inc(sy.dma_start(
                    dst.ap().rearrange("p (n c) -> p n c", c=32),
                    src.ap().rearrange("(n p) c -> p n c", p=128)), "sH", 16)

        @blk.gpsimd
        def _(g):
            for grp in range(NG):
                W.inc(g.memset(
                    bass.AP(Vt, grp * VST + DH,
                            [[Vt.shape[1], 128], [DH + 1, S // 128], [1, 1]]), 1.0), "sG", 1)
        m_p0 = W.snap()

        p12 = ExitStack()
        wstg = [self.sb(p12, f"wstg{i}", [128, D], BF16, side="right") for i in range(2)]
        xt = [self.sb(p12, f"xt{i}", [128, D], F32, side="right") for i in range(2)]
        stt = [self.sb(p12, f"stt{i}", [128, 8], F32, side="right") for i in range(2)]
        sqjunk = self.sb(p12, "sqjunk", [128, D], BF16, side="right")
        xrs = self.sb(p12, "xrs", [128, D], F32, side="right")
        xrb = [self.sb(p12, f"xrb{i}", [128, D], BF16, side="right") for i in range(2)]
        TB = 512
        xrf = self.sb(p12, "xrf", [128, FC * TB], BF16, side="right")
        ropeA = self.sb(p12, "ropeA", [128, (max(KCOLS, 512) // DH) * 32], F32, side="right")
        ropeB = self.sb(p12, "ropeB", [128, (max(KCOLS, 512) // DH) * 32], F32, side="right")
        pmm = [self.ps(p12, f"pmm{i}", [128, 512], F32) for i in range(2)]

        def emit_ln_tile(xtb, stb, junkb, dst_bf, with_rms, wait0, wait_dst=None,
                         save_tt=None):
            st = stb

            @blk.scalar
            def _(sc):
                c = _Chained(W, sc, "sA")
                W.wge(sc, *wait0)
                c.activation(junkb[:, :], xtb[:, :], AF.Square, accum_out=st[:, 4:5])
            m_sq = W.snap()

            @blk.vector
            def _(v):
                c = _Chained(W, v, "sV")
                W.wge(v, *wait0)
                c.tensor_reduce(st[:, 3:4], xtb[:, :], AX, Alu.add)
                c.tensor_scalar(st[:, 0:1], st[:, 3:4], 1.0 / D, None, Alu.mult)
                W.wge(v, m_sq)
                c.tensor_scalar(st[:, 5:6], st[:, 4:5], 1.0 / D, None, Alu.mult)
                c.tensor_scalar(st[:, 6:7], st[:, 0:1], st[:, 0:1], None, Alu.mult)
                c.tensor_tensor(st[:, 6:7], st[:, 5:6], st[:, 6:7], Alu.subtract)
                if with_rms:
                    c.tensor_scalar(st[:, 5:6], st[:, 6:7], 1.0 + 1e-6, 1e-11,
                                    Alu.mult, Alu.add)
                    c.tensor_scalar(st[:, 6:7], st[:, 6:7], 1.0, 1e-5, Alu.mult, Alu.add)
                else:
                    c.tensor_scalar(st[:, 5:6], st[:, 6:7], 1.0, 1e-5, Alu.mult, Alu.add)
            m_v1 = W.snap()

            @blk.scalar
            def _(sc):
                c = _Chained(W, sc, "sA")
                W.wge(sc, m_v1)
                n = 2 if with_rms else 1
                c.activation(st[:, 5:5 + n], st[:, 5:5 + n], AF.Sqrt)
            m_a2 = W.snap()

            @blk.vector
            def _(v):
                c = _Chained(W, v, "sV")
                W.wge(v, m_a2)
                if wait_dst:
                    W.wge(v, *wait_dst)
                n = 2 if with_rms else 1
                c.reciprocal(st[:, 1:1 + n], st[:, 5:5 + n])
                if with_rms:
                    c.tensor_scalar(xrs[:, :], xtb[:, :], st[:, 0:1], st[:, 1:2],
                                    Alu.subtract, Alu.mult)
                    c.tensor_tensor(dst_bf[:, :], xrs[:, :], rmsw[:, :], Alu.mult)
                else:
                    c.tensor_scalar(dst_bf[:, :], xtb[:, :], st[:, 0:1], st[:, 1:2],
                                    Alu.subtract, Alu.mult)
                if save_tt is not None:
                    c.tensor_copy(muR[:, 2 * save_tt:2 * save_tt + 1], st[:, 0:1])
                    c.tensor_copy(muR[:, 2 * save_tt + 1:2 * save_tt + 2], st[:, 1:2])
            return W.snap()

        def emit_rope(pb, cos_sb, sin_sb, ti, ncols, out_bf, wait0):
            ngc = ncols // DH

            @blk.vector
            def _(v):
                c = _Chained(W, v, "sV")
                W.wge(v, *wait0)
                ct = cos_sb[:, 32 * ti:32 * (ti + 1)].unsqueeze(1).broadcast_to([128, ngc, 32])
                st_ = sin_sb[:, 32 * ti:32 * (ti + 1)].unsqueeze(1).broadcast_to([128, ngc, 32])
                x1 = bass.AP(pb, 0, [[512, 128], [DH, ngc], [1, 32]])
                x2 = bass.AP(pb, 32, [[512, 128], [DH, ngc], [1, 32]])
                a = ropeA.ap().rearrange("p (g c) -> p g c", c=32)[:, 0:ngc, :]
                b = ropeB.ap().rearrange("p (g c) -> p g c", c=32)[:, 0:ngc, :]
                o = out_bf.ap()[:, 0:ncols].rearrange("p (g two c) -> p g two c", two=2, c=32)
                c.tensor_tensor(a, x1, ct, Alu.mult)
                c.tensor_tensor(b, x2, st_, Alu.mult)
                c.tensor_tensor(o[:, :, 0, :], a, b, Alu.subtract)
                c.tensor_tensor(a, x2, ct, Alu.mult)
                c.tensor_tensor(b, x1, st_, Alu.mult)
                c.tensor_tensor(o[:, :, 1, :], a, b, Alu.add)
            return W.snap()

        p1 = ExitStack()
        NKV = 2 * KCOLS
        wkvT = self.sb(p1, "wkvT", [128, FC * NKV], BF16, side="right")
        ktm = [self.sb(p1, f"ktm{i}", [128, KCOLS], BF16, side="right") for i in range(2)]

        m_wstg_free = [m_p0, m_p0]
        m_wstg_ld = [None, None]
        for ch in range(NKV // 128):
            r = ch % 2

            @blk.gpsimd
            def _(g, ch=ch, r=r, mkf=m_wstg_free[r]):
                W.wge(g, mkf)
                W.inc(g.dma_start(wstg[r][:, :],
                                  w_in[QCOLS + 128 * ch: QCOLS + 128 * (ch + 1), :]), "sL", 16)
            m_wstg_ld[r] = W.snap()

            @blk.sync
            def _(sy, ch=ch, r=r, mk=m_wstg_ld[r]):
                W.wge(sy, mk)
                W.inc(sy.dma_start(
                    bass.AP(wkvT, 128 * ch, [[FC * NKV, 128], [NKV, FC], [1, 128]]),
                    wstg[r].ap().rearrange("t (j p) -> t j p", p=128),
                    transpose=True), "sH", 16)
            m_wstg_free[r] = W.snap()
        m_wkvT = W.snap()

        NKVC = (NKV + 511) // 512
        TPB = TB // 128
        m_ln_done = [m_p0, m_p0]
        m_xrb_free = [m_p0, m_p0]
        m_xld = [None, None]
        m_mm_free = [m_p0, m_p0]
        m_ktm_free = [m_p0, m_p0]
        m_xrf_free = m_p0

        for tb in range(S // TB):
            for tt in range(TPB):
                ti = tb * TPB + tt
                r = ti % 2

                @blk.sync
                def _(sy, ti=ti, r=r, mkf=m_ln_done[r]):
                    W.wge(sy, mkf)
                    W.inc(sy.dma_start(xt[r][:, :], x_seq[128 * ti:128 * (ti + 1), :]),
                          "sH", 16)
                m_xld[r] = W.snap()

                m_ln_done[r] = emit_ln_tile(xt[r], stt[r], sqjunk, xrb[r], True,
                                            [m_xld[r]], [m_xrb_free[r]])

                @blk.sync
                def _(sy, tt=tt, r=r, mk=m_ln_done[r], mkx=m_xrf_free):
                    W.wge(sy, mk, mkx)
                    W.inc(sy.dma_start(
                        bass.AP(xrf, 128 * tt, [[FC * TB, 128], [TB, FC], [1, 128]]),
                        xrb[r].ap().rearrange("t (j p) -> t j p", p=128),
                        transpose=True), "sH", 16)
                m_xrb_free[r] = W.snap()
            m_batchT = W.snap()

            for kc in range(NKVC):
                ncols = min(512, NKV - 512 * kc)
                for tt in range(TPB):
                    ti = tb * TPB + tt
                    p2i = (kc * TPB + tt) % 2
                    pb = pmm[p2i]

                    @blk.tensor
                    def _(t, kc=kc, tt=tt, pb=pb, ncols=ncols,
                          mkb=m_batchT, mkf=m_mm_free[p2i]):
                        W.wge(t, mkb, m_wkvT, mkf)
                        for fc in range(FC):
                            W.inc(t.matmul(pb[0:128, 0:ncols],
                                           bass.AP(xrf, fc * TB + 128 * tt,
                                                   [[FC * TB, 128], [1, 128]]),
                                           bass.AP(wkvT, fc * NKV + 512 * kc,
                                                   [[FC * NKV, 128], [1, ncols]]),
                                           start=(fc == 0), stop=(fc == FC - 1)), "sT", 1)
                    m_mm = W.snap()

                    if 512 * kc < KCOLS:
                        kr = ti % 2
                        emit_rope(pb, kcos, ksin, ti, ncols, ktm[kr],
                                  [m_mm, m_ktm_free[kr]])
                        m_mm_free[p2i] = W.snap()
                        m_rope = W.snap()

                        @blk.sync
                        def _(sy, ti=ti, kr=kr, ncols=ncols, mk=m_rope):
                            W.wge(sy, mk)
                            W.inc(sy.dma_start(
                                bass.AP(K2, 128 * ti, [[NPAIR * S, 128], [S, ncols // 128], [1, 128]]),
                                ktm[kr].ap()[:, 0:ncols].rearrange("t (j p) -> t j p", p=128),
                                transpose=True), "sH", 16)
                        m_ktm_free[kr] = W.snap()
                    else:
                        @blk.vector
                        def _(v, pb=pb, ti=ti, ncols=ncols, kc=kc, mk=m_mm):
                            c = _Chained(W, v, "sV")
                            W.wge(v, mk)
                            ngc = ncols // DH
                            g0 = (512 * kc - KCOLS) // DH
                            src = bass.AP(pb, 0, [[512, 128], [DH, ngc], [1, DH]])
                            dst = bass.AP(Vt, g0 * VST + ti * (DH + 1),
                                          [[Vt.shape[1], 128], [VST, ngc], [1, DH]])
                            c.tensor_copy(dst, src)
                        m_mm_free[p2i] = W.snap()
            m_xrf_free = W.snap()

        p1.close()
        p2 = ExitStack()
        Qp = self.sb(pQp, "Qp", [128, NPAIR * HPG * OWN], BF16)
        QPW = Qp.shape[1]
        wqT = self.sb(p2, "wqT", [128, FC * 512], BF16, side="right")
        qtm1 = self.sb(p2, "qtm", [128, 512], BF16, side="right")
        qtm = [qtm1, qtm1]
        qstg1 = self.sb(p2, "qstg", [128, 512], BF16, side="right")
        qstg = [qstg1, qstg1]
        OH = max(1, OT // 2)
        m_qtm_free = [W.snap(), W.snap()]
        m_qstg_free = [W.snap(), W.snap()]
        m_wqT_free = W.snap()

        for half in range((OT + OH - 1) // OH):
            tiles = list(range(half * OH, min(OT, (half + 1) * OH)))
            for tt in tiles:
                r = tt % 2

                @blk.sync
                def _(sy, tt=tt, r=r, mkf=m_ln_done[r]):
                    W.wge(sy, mkf)
                    W.inc(sy.dma_start(xt[r][:, :], x_own[128 * tt:128 * (tt + 1), :]),
                          "sH", 16)
                m_xld[r] = W.snap()

                m_ln_done[r] = emit_ln_tile(xt[r], stt[r], sqjunk, xrb[r], True,
                                            [m_xld[r]], [m_xrb_free[r]])

                @blk.sync
                def _(sy, tt=tt, r=r, half=half, mk=m_ln_done[r], mkx=m_xrf_free):
                    W.wge(sy, mk, mkx)
                    W.inc(sy.dma_start(
                        bass.AP(xrf, 128 * (tt - half * OH), [[FC * TB, 128], [TB, FC], [1, 128]]),
                        xrb[r].ap().rearrange("t (j p) -> t j p", p=128),
                        transpose=True), "sH", 16)
                m_xrb_free[r] = W.snap()
            m_batchT = W.snap()

            for qc in range(QCOLS // 512):
                for i in range(4):
                    r = i % 2

                    @blk.gpsimd
                    def _(g, qc=qc, i=i, r=r, mkf=m_wstg_free[r]):
                        W.wge(g, mkf)
                        W.inc(g.dma_start(wstg[r][:, :],
                                          w_in[512 * qc + 128 * i: 512 * qc + 128 * (i + 1), :]),
                              "sL", 16)
                    m_wstg_ld[r] = W.snap()

                    @blk.sync
                    def _(sy, i=i, r=r, mk=m_wstg_ld[r], mkw=m_wqT_free):
                        W.wge(sy, mk, mkw)
                        W.inc(sy.dma_start(
                            bass.AP(wqT, 128 * i, [[FC * 512, 128], [512, FC], [1, 128]]),
                            wstg[r].ap().rearrange("t (j p) -> t j p", p=128),
                            transpose=True), "sH", 16)
                    m_wstg_free[r] = W.snap()
                m_wqTT = W.snap()

                for tt in tiles:
                    p2i = tt % 2
                    pb = pmm[p2i]

                    @blk.tensor
                    def _(t, tt=tt, pb=pb, half=half, mkb=m_batchT, mkw=m_wqTT,
                          mkf=m_mm_free[p2i]):
                        W.wge(t, mkb, mkw, mkf)
                        for fc in range(FC):
                            W.inc(t.matmul(pb[0:128, 0:512],
                                           bass.AP(xrf, fc * TB + 128 * (tt - half * OH),
                                                   [[FC * TB, 128], [1, 128]]),
                                           bass.AP(wqT, fc * 512, [[FC * 512, 128], [1, 512]]),
                                           start=(fc == 0), stop=(fc == FC - 1)), "sT", 1)
                    m_mm = W.snap()

                    qr = tt % 2
                    emit_rope(pb, qcos, qsin, tt, 512, qtm[qr],
                              [m_mm, m_qtm_free[qr]])
                    m_mm_free[p2i] = W.snap()
                    m_rope = W.snap()

                    @blk.sync
                    def _(sy, qr=qr, mk=m_rope, mkq=m_qstg_free[qr]):
                        W.wge(sy, mk, mkq)
                        W.inc(sy.dma_start(
                            qstg[qr].ap().rearrange("p (j t) -> p j t", t=128),
                            qtm[qr].ap().rearrange("t (j p) -> t j p", p=128),
                            transpose=True), "sH", 16)
                    m_qtm_free[qr] = W.snap()
                    m_qstgT = W.snap()

                    @blk.sync
                    def _(sy, qc=qc, tt=tt, qr=qr, mk=m_qstgT):
                        W.wge(sy, mk)
                        for hh in range(8):
                            h = 8 * qc + hh
                            g_ = h // HPG
                            hi = h % HPG
                            jb, srch = hh // 2, hh % 2
                            dst = bass.AP(Qp, (g_ // 2) * HPG * OWN + hi * OWN + 128 * tt
                                          + (64 * (g_ % 2)) * QPW, [[QPW, 64], [1, 128]])
                            src = bass.AP(qstg[qr], 128 * jb + (64 * srch) * qstg[qr].shape[1],
                                          [[qstg[qr].shape[1], 64], [1, 128]])
                            W.inc(sy.dma_start(dst, src), "sH", 16)
                    m_qstg_free[qr] = W.snap()
                m_wqT_free = W.snap()

        if cfg.debug:
            @blk.gpsimd
            def _(g):
                W.w(g)
                W.inc(g.dma_start(dbg_K2.ap(), K2.ap()), "sL", 16)
                W.inc(g.dma_start(dbg_Vt.ap(), Vt.ap()), "sL", 16)
                W.inc(g.dma_start(dbg_Qp.ap(), Qp.ap()), "sL", 16)

        p2.close()
        p12.close()
        p0.close()

        p3F = ExitStack()
        o_fm = self.sb(p3F, "o_fm", [128, CC * OWN], BF16, side="right")
        OFW = o_fm.shape[1]
        p3 = ExitStack()
        ejunk = self.sb(p3, "ejunk", [128, SEG], BF16, side="right")
        Pb2 = [self.sb(p3, f"Pb2_{i}", [128, 2 * SEG], BF16, side="right")
               for i in range(2)]
        PT = [self.sb(p3, f"PT{i}", [128, 2 * SEGB * HPG * QT], BF16, side="right")
              for i in range(2)]
        PTW = PT[0].shape[1]
        NSTR = 8
        strip1 = self.sb(p3, "strip1", [QT, 2 * HPG * NSTR], F32, side="right")
        strip2 = self.sb(p3, "strip2", [QT, 2 * HPG * NSTR], F32, side="right")
        bias_eo = self.sb(p3, "bias_eo", [QT, 2 * HPG], F32, side="right")
        rsum = self.sb(p3, "rsum", [QT, 2 * HPG], F32, side="right")
        rrec = self.sb(p3, "rrec", [QT, 2 * HPG], F32, side="right")
        rrow = self.sb(p3, "rrow", [1, 2 * HPG * QT], F32, side="right")
        bcast = self.sb(p3, "bcast", [128, HPG * QT], F32, side="right")
        o_stg = self.sb(p3, "o_stg", [128, HPG * QT], BF16, side="right")
        psS = [self.ps(p3, f"psS{i}", [128, 2 * SEG], F32) for i in range(2)]
        psOe = self.ps(p3, "psOe", [128, HPG * QT], F32)
        psOo = self.ps(p3, "psOo", [128, HPG * QT], F32)
        psR = self.ps(p3, "psR", [128, HPG * QT], F32)
        psB = self.ps(p3, "psB", [128, HPG * QT], F32)

        def slot_ap(sd, par):
            if sd[0] == "pair":
                return sd[1][0:QT, par * SEG:(par + 1) * SEG]
            return sd[1 + par][0:QT, 0:SEG]

        def scores_mms(t, gp, qcol, seg, sd, masked):
            for par in range(2):
                dst = slot_ap(sd, par)
                W.inc(t.matmul(dst,
                               Qp[64 * par:64 * par + 64, qcol:qcol + QT],
                               K2[64 * par:64 * par + 64,
                                  gp * S + SEG * seg: gp * S + SEG * (seg + 1)],
                               start=True, stop=True), "sT", 1)
                if masked:
                    W.inc(t.matmul(dst,
                                   identb[0:QT, 0:QT], tmaskb[0:QT, 0:SEG],
                                   start=False, stop=True,
                                   skip_group_check=True), "sT", 1)

        m_p3start = W.snap()
        m_psO_free = W.snap()
        m_v1 = W.snap()
        m_prev_tr = W.snap()
        m_prev_av = W.snap()
        m_s2_free = W.snap()

        for j in range(NSLOT):
            nseg = j + 1
            nI = 4 * nseg
            for gp in range(NPAIR):
                A0 = W.c["sA"]
                V0 = W.c["sV"]
                T0 = W.c["sT"]
                SL1 = [("pair", psS[0]), ("pair", psS[1]),
                       ("two", psOe, psOo), ("two", psB, psR)]
                D1 = len(SL1)
                scT = []
                redV = []
                tacc = T0
                vacc = V0
                for i in range(nI):
                    si = i % nseg
                    tacc += 4 if si == nseg - 1 else 2
                    scT.append(tacc)
                    vacc += 1 if SL1[i % D1][0] == "pair" else 2
                    redV.append(vacc)

                @blk.tensor
                def _(t, j=j, gp=gp, nseg=nseg, nI=nI, A0=A0, V0=V0, redV=redV,
                      SL1=SL1, D1=D1, mps=m_p3start):
                    W.wge(t, mps)
                    t.wait_ge(W.sem["sA"], A0)
                    t.wait_ge(W.sem["sV"], V0)
                    for i in range(nI):
                        hi, si = divmod(i, nseg)
                        if i >= D1:
                            t.wait_ge(W.sem["sV"], redV[i - D1])
                        scores_mms(t, gp, gp * HPG * OWN + hi * OWN + j * QT,
                                   si, SL1[i % D1], si == nseg - 1)

                @blk.vector
                def _(v, nseg=nseg, nI=nI, scT=scT, SL1=SL1, D1=D1, mv=m_v1):
                    W.wge(v, mv)
                    for i in range(nI):
                        hi, si = divmod(i, nseg)
                        v.wait_ge(W.sem["sT"], scT[i])
                        col = NSTR * 2 * hi + si
                        sd = SL1[i % D1]
                        if sd[0] == "pair":
                            sp = sd[1].ap()[0:QT, :].rearrange(
                                "q (two k) -> q two k", two=2)
                            dst = bass.AP(strip1, col,
                                          [[strip1.shape[1], QT], [NSTR, 2]])
                            W.inc(v.tensor_reduce(dst, sp, AX, Alu.max), "sV", 1)
                        else:
                            for par in range(2):
                                W.inc(v.tensor_reduce(
                                    strip1[0:QT, col + NSTR * par:
                                           col + NSTR * par + 1],
                                    sd[1 + par][0:QT, 0:SEG], AX, Alu.max),
                                    "sV", 1)
                m_p1 = W.snap()

                @blk.vector
                def _(v, nseg=nseg, mk=m_p1):
                    c = _Chained(W, v, "sV")
                    W.wge(v, mk)
                    for cc_ in range(2 * HPG):
                        c.tensor_reduce(rsum[0:QT, cc_:cc_ + 1],
                                        strip1[0:QT, NSTR * cc_:NSTR * cc_ + nseg],
                                        AX, Alu.max)
                    c.tensor_scalar(bias_eo[0:QT, :], rsum[0:QT, :], -0.125, None,
                                    Alu.mult)
                m_v1 = W.snap()
                m_bias = W.snap()

                if cfg.debug and j == 0 and gp == 0:
                    @blk.gpsimd
                    def _(g):
                        W.w(g)
                        W.inc(g.dma_start(dbg_s1.ap(), strip1.ap()), "sL", 16)
                        W.inc(g.dma_start(dbg_be.ap(), bias_eo.ap()), "sL", 16)

                A0 = W.c["sA"]
                T0 = W.c["sT"]
                H0 = W.c["sH"]
                V_bias = W.c["sV"]
                order = []
                for sseg in range(nseg):
                    order.append(("S", sseg))
                    if sseg >= 1:
                        order.append(("A", sseg - 1))
                order.append(("A", nseg - 1))
                scT2 = [0] * nI
                avT = [0] * nseg
                tacc = T0
                SL2 = [("pair", psS[0]), ("pair", psS[1]), ("two", psB, psR)]
                D2 = len(SL2)
                for kind, sseg in order:
                    if kind == "S":
                        for hi in range(4):
                            tacc += 4 if sseg == nseg - 1 else 2
                            scT2[4 * sseg + hi] = tacc
                    else:
                        tacc += 2 * SEGB
                        avT[sseg] = tacc
                expA = [A0 + 2 * (i + 1) for i in range(nI)]
                trH = [H0 + 16 * (i + 1) for i in range(nI)]

                @blk.tensor
                def _(t, j=j, gp=gp, nseg=nseg, A0=A0, V_bias=V_bias, expA=expA,
                      SL2=SL2, D2=D2, order=order, trH=trH, mo=m_psO_free):
                    W.wge(t, mo)
                    t.wait_ge(W.sem["sA"], A0)
                    t.wait_ge(W.sem["sV"], V_bias)
                    for kind, sseg in order:
                        if kind == "S":
                            for hi in range(4):
                                i = 4 * sseg + hi
                                if i >= D2:
                                    t.wait_ge(W.sem["sA"], expA[i - D2])
                                scores_mms(t, gp,
                                           gp * HPG * OWN + hi * OWN + j * QT,
                                           sseg, SL2[i % D2], sseg == nseg - 1)
                        else:
                            t.wait_ge(W.sem["sH"], trH[4 * sseg + 3])
                            for kb in range(SEGB):
                                first = (sseg == 0 and kb == 0)
                                last = (sseg == nseg - 1 and kb == SEGB - 1)
                                kti = sseg * SEGB + kb
                                W.inc(t.matmul(psOe[0:64, 0:HPG * QT],
                                               Vt[:, (2 * gp) * VST + kti * (DH + 1):
                                                  (2 * gp) * VST + kti * (DH + 1) + DH],
                                               PT[sseg % 2][:, kb * HPG * QT:
                                                            (kb + 1) * HPG * QT],
                                               start=first, stop=last,
                                               tile_position=(0, 0),
                                               skip_group_check=True), "sT", 1)
                                W.inc(t.matmul(psOo[64:128, 0:HPG * QT],
                                               Vt[:, (2 * gp + 1) * VST + kti * (DH + 1):
                                                  (2 * gp + 1) * VST + kti * (DH + 1) + DH],
                                               PT[sseg % 2][:, SEGB * HPG * QT + kb * HPG * QT:
                                                            SEGB * HPG * QT + (kb + 1) * HPG * QT],
                                               start=first, stop=last,
                                               tile_position=(0, 64),
                                               skip_group_check=True), "sT", 1)
                m_avdone = W.snap()

                @blk.scalar
                def _(sc, nseg=nseg, nI=nI, scT2=scT2, trH=trH, V_bias=V_bias,
                      SL2=SL2, D2=D2, mtr=m_prev_tr, ms2=m_s2_free):
                    W.wge(sc, mtr, ms2)
                    sc.wait_ge(W.sem["sV"], V_bias)
                    for i in range(nI):
                        sseg, hi = divmod(i, 4)
                        sc.wait_ge(W.sem["sT"], scT2[i])
                        if i >= 2:
                            sc.wait_ge(W.sem["sH"], trH[i - 2])
                        for par in range(2):
                            col = NSTR * (2 * hi + par) + sseg
                            W.inc(sc.activation(
                                Pb2[i % 2][0:QT, par * SEG:(par + 1) * SEG],
                                slot_ap(SL2[i % D2], par),
                                AF.Exp, scale=cAP(C_P125, QT),
                                bias=bias_eo[0:QT, 2 * hi + par:2 * hi + par + 1],
                                accum_out=strip2[0:QT, col:col + 1]), "sA", 1)
                m_p2 = W.snap()

                @blk.sync
                def _(sy, nseg=nseg, nI=nI, expA=expA, avT=avT, mpav=m_prev_av):
                    W.wge(sy, mpav)
                    for i in range(nI):
                        sseg = i // 4
                        sy.wait_ge(W.sem["sA"], expA[i])
                        if i % 4 == 0 and sseg >= 2:
                            sy.wait_ge(W.sem["sT"], avT[sseg - 2])
                        W.inc(sy.dma_start(
                            bass.AP(PT[sseg % 2], (i % 4) * QT,
                                    [[PTW, 128], [HPG * QT, 2 * SEGB], [1, QT]]),
                            Pb2[i % 2].ap()[0:QT, 0:2 * SEG]
                              .rearrange("t (j p) -> t j p", p=128),
                            transpose=True), "sH", 16)
                m_prev_tr = W.snap()
                m_prev_av = m_avdone

                @blk.vector
                def _(v, nseg=nseg, mk=m_p2):
                    c = _Chained(W, v, "sV")
                    W.wge(v, mk)
                    for cc_ in range(2 * HPG):
                        c.tensor_reduce(rsum[0:QT, cc_:cc_ + 1],
                                        strip2[0:QT, NSTR * cc_:NSTR * cc_ + nseg],
                                        AX, Alu.add)
                    c.reciprocal(rrec[0:QT, :], rsum[0:QT, :])
                m_s2_free = W.snap()
                m_rr = W.snap()

                m_rrow = None
                for rnd in range(2):
                    @blk.tensor
                    def _(t, rnd=rnd, mk=m_rr, mkc=m_rrow):
                        W.wge(t, mk, mkc)
                        for cc_ in range(HPG):
                            W.inc(t.matmul(psR[0:1, QT * cc_:QT * (cc_ + 1)],
                                           rrec[0:QT, HPG * rnd + cc_:HPG * rnd + cc_ + 1],
                                           ident_f[0:QT, 0:QT],
                                           start=True, stop=True), "sT", 1)
                    m_psr = W.snap()

                    @blk.vector
                    def _(v, rnd=rnd, mk=m_psr):
                        c = _Chained(W, v, "sV")
                        W.wge(v, mk)
                        c.tensor_copy(rrow[0:1, HPG * QT * rnd:HPG * QT * (rnd + 1)],
                                      psR[0:1, 0:HPG * QT])
                    m_rrow = W.snap()

                @blk.tensor
                def _(t, mk=m_rrow):
                    W.wge(t, mk)
                    for par in range(2):
                        W.inc(t.matmul(psB[64 * par:64 * par + 64, 0:HPG * QT],
                                       ones_f[0:1, 0:64],
                                       bass.AP(rrow, QT * par, [[rrow.shape[1], 1],
                                                               [2 * QT, HPG], [1, QT]]),
                                       start=True, stop=True,
                                       tile_position=(0, 64 * par)), "sT", 1)
                m_psb = W.snap()

                @blk.vector
                def _(v, mk=m_psb, mkav=m_avdone):
                    c = _Chained(W, v, "sV")
                    W.wge(v, mk, mkav)
                    c.tensor_copy(bcast[:, :], psB[0:128, :])
                    c.tensor_tensor(o_stg[0:64, :], psOe[0:64, :], bcast[0:64, :], Alu.mult)
                    c.tensor_tensor(o_stg[64:128, :], psOo[64:128, :],
                                    bcast[64:128, :], Alu.mult)
                m_psO_free = W.snap()
                m_ostg = W.snap()

                @blk.sync
                def _(sy, j=j, gp=gp, mk=m_ostg):
                    W.wge(sy, mk)
                    for par in range(2):
                        g_ = 2 * gp + par
                        for hi in range(HPG):
                            cch = (HPG * g_ + hi) // 2
                            rh = 64 * ((HPG * g_ + hi) % 2)
                            d = bass.AP(o_fm, cch * OWN + j * QT + rh * OFW, [[OFW, 64], [1, QT]])
                            s_ = bass.AP(o_stg, hi * QT + (64 * par) * o_stg.shape[1],
                                         [[o_stg.shape[1], 64], [1, QT]])
                            W.inc(sy.dma_start(d, s_), "sH", 16)

                if cfg.debug and j == 0 and gp == 0:
                    @blk.gpsimd
                    def _(g):
                        W.w(g)
                        W.inc(g.dma_start(dbg_Pb0.ap(), Pb2[0][:, 0:SEG]), "sL", 16)
                        W.inc(g.dma_start(dbg_s2.ap(), strip2.ap()), "sL", 16)
                        W.inc(g.dma_start(dbg_rr.ap(), rrec.ap()), "sL", 16)
                        W.inc(g.dma_start(dbg_ost.ap(), o_stg.ap()), "sL", 16)
                        W.inc(g.dma_start(dbg_bc.ap(), bcast.ap()), "sL", 16)

        self.serial = False
        if cfg.debug:
            @blk.gpsimd
            def _(g):
                W.w(g)
                W.inc(g.dma_start(dbg_ofm.ap(), o_fm.ap()), "sL", 16)

        p3.close()
        pQp.close()
        patt.close()
        p4 = ExitStack()
        woT = self.sb(p4, "woT", [128, CC * 512], BF16, side="right")
        wstg4 = [self.sb(p4, f"wstg4_{i}", [128, QCOLS], BF16, side="right") for i in range(2)]
        ysb = [self.sb(p4, f"ysb{i}", [128, 512], F32, side="right") for i in range(2)]
        psY = [self.ps(p4, f"psY{i}", [128, 512], F32) for i in range(2)]
        m_w4_free = [W.snap(), W.snap()]
        m_woT_free = W.snap()
        m_psY_free = [W.snap(), W.snap()]
        m_ysb_free = [W.snap(), W.snap()]

        for mb in range(D // 512):
            for i in range(4):
                r = i % 2

                @blk.gpsimd
                def _(g, mb=mb, i=i, r=r, mkf=m_w4_free[r]):
                    W.wge(g, mkf)
                    W.inc(g.dma_start(wstg4[r][:, :],
                                      w_out[512 * mb + 128 * i: 512 * mb + 128 * (i + 1), :]),
                          "sL", 16)
                m_ld = W.snap()

                @blk.sync
                def _(sy, i=i, r=r, mk=m_ld, mkw=m_woT_free):
                    W.wge(sy, mk, mkw)
                    W.inc(sy.dma_start(
                        bass.AP(woT, 128 * i, [[CC * 512, 128], [512, CC], [1, 128]]),
                        wstg4[r].ap().rearrange("t (j p) -> t j p", p=128),
                        transpose=True), "sH", 16)
                m_w4_free[r] = W.snap()
            m_woTT = W.snap()

            for tt in range(OT):
                p4i = tt % 2
                pb = psY[p4i]

                @blk.tensor
                def _(t, tt=tt, pb=pb, mk=m_woTT, mkf=m_psY_free[p4i]):
                    W.wge(t, mk, mkf)
                    for cc_ in range(CC):
                        W.inc(t.matmul(pb[0:128, 0:512],
                                       o_fm[:, cc_ * OWN + 128 * tt: cc_ * OWN + 128 * (tt + 1)],
                                       woT[:, cc_ * 512:(cc_ + 1) * 512],
                                       start=(cc_ == 0), stop=(cc_ == CC - 1)), "sT", 1)
                m_mm = W.snap()

                @blk.vector
                def _(v, pb=pb, p4i=p4i, mk=m_mm, mky=m_ysb_free[p4i]):
                    c = _Chained(W, v, "sV")
                    W.wge(v, mk, mky)
                    c.tensor_copy(ysb[p4i][:, :], pb[0:128, 0:512])
                m_psY_free[p4i] = W.snap()
                m_ev = W.snap()

                @blk.sync
                def _(sy, mb=mb, tt=tt, p4i=p4i, mk=m_ev):
                    W.wge(sy, mk)
                    W.inc(sy.dma_start(y_spill[128 * tt:128 * (tt + 1), 512 * mb:512 * (mb + 1)],
                                       ysb[p4i][:, :]), "sH", 16)
                m_ysb_free[p4i] = W.snap()
            m_woT_free = W.snap()

        p4.close()
        p3F.close()

        m_P5start = W.snap()
        p5 = ExitStack()
        p5X = ExitStack()
        xnf = self.sb(p5X, "xnf", [128, FC * OWN], BF16)
        gacc = self.sb(p5X, "gacc", [128, OT * D], F32)
        muI = self.sb(p5, "muI", [128, max(OT, 8)], F32, side="right")
        p5W = ExitStack()
        xt5 = self.sb(p5W, "xt5", [128, D], F32, side="right")
        st5 = self.sb(p5W, "st5", [128, 8], F32, side="right")
        qscrV = self.sb(p5W, "qscrV", [128, 64], F32, side="right")
        qsbV = self.sb(p5W, "qsbV", [128, 32], BF16, side="right")
        mch = self.sb(p5W, "mch", [128, max(D, 1024)], BF16, side="right")
        hring = self.sb(p5W, "hring", [128, 8 * OWN], BF16, side="right")
        wg_n = [self.sb(p5W, f"wg_n{i}", [128, D], BF16, side="right") for i in range(3)]
        wu_n = [self.sb(p5W, f"wu_n{i}", [128, D], BF16, side="right") for i in range(3)]
        wgT = [self.sb(p5W, f"wgT{i}", [128, FC * 128], BF16, side="right") for i in range(2)]
        wuT = [self.sb(p5W, f"wuT{i}", [128, FC * 128], BF16, side="right") for i in range(2)]
        THALF = min(512, OWN)
        NHALF = OWN // THALF
        silscr = [self.sb(p5W, f"silscr{i}", [128, THALF], F32, side="right")
                  for i in range(2)]
        sqh = self.sb(p5W, "sqh", [128, THALF], BF16, side="right")
        wp_n = [self.sb(p5W, f"wp_n{i}", [128, 1024], BF16, side="right") for i in range(2)]
        wpT = self.sb(p5W, "wpT", [128, 8 * D], BF16, side="right")
        psG = [self.ps(p5W, f"psG{i}", [128, THALF], F32) for i in range(2)]
        psU = [self.ps(p5W, f"psU{i}", [128, THALF], F32) for i in range(2)]
        psT5 = self.ps(p5W, "psT5", [128, 512], F32)
        psP = [self.ps(p5W, f"psP{i}", [128, 512], F32) for i in range(2)]
        NSUP = HF // 1024
        NDC = D // 512
        m_qB = {"B": None}
        xnb5 = hring[:, D:2 * D]

        def emit_quant(src_bf, dst_bf, Fw, wait0):
            ng = Fw // 64

            @blk.vector
            def _(v, mkb=m_qB["B"]):
                c = _Chained(W, v, "sV")
                W.wge(v, mkb, *wait0)
                c.tensor_reduce(qscrV[:, 0:ng],
                                src_bf.ap()[:, 0:Fw].rearrange("p (g k) -> p g k", k=64),
                                AX, Alu.add, apply_absolute_value=True)
                c.tensor_scalar(qsbV[:, 0:ng], qscrV[:, 0:ng], 1.0 / 64, 1e-8,
                                Alu.mult, Alu.max)
                c.tensor_scalar(qscrV[:, 32:32 + ng], qsbV[:, 0:ng], THR_BF16, None,
                                Alu.mult)
            m_s = W.snap()

            halves = ((blk.vector, "sV", 0), (blk.gpsimd, "sG", 1)) if POOL_QUANT \
                else ((blk.vector, "sV", 0),)
            nh = len(halves)
            Fh = Fw // nh
            ngh = ng // nh
            for eng_dec, semn, hx in halves:
                @eng_dec
                def _(v, semn=semn, hx=hx, mk=m_s):
                    c = _Chained(W, v, semn)
                    W.wge(v, mk)
                    c0 = hx * Fh
                    g0 = hx * ngh
                    srcr = src_bf.ap()[:, c0:c0 + Fh].rearrange("p (g k) -> p g k", k=64)
                    dstr = dst_bf.ap()[:, c0:c0 + Fh].rearrange("p (g k) -> p g k", k=64)
                    mr = mch.ap()[:, c0:c0 + Fh].rearrange("p (g k) -> p g k", k=64)
                    thr_b = qscrV[:, 32 + g0:32 + g0 + ngh].unsqueeze(2) \
                        .broadcast_to([128, ngh, 64])
                    s_b = qsbV[:, g0:g0 + ngh].unsqueeze(2).broadcast_to([128, ngh, 64])
                    c.scalar_tensor_tensor(mr, srcr, cAP(C_NEG1), thr_b,
                                           Alu.mult, Alu.is_gt)
                    c.tensor_tensor(dstr, srcr, thr_b, Alu.is_gt)
                    c.tensor_tensor(dstr, dstr, mr, Alu.subtract)
                    c.tensor_tensor(dstr, dstr, s_b, Alu.mult)
            m_qB["B"] = W.snap()
            return m_qB["B"]

        m_ln5_done = W.snap()
        m_xnb5_free = W.snap()
        for tt in range(OT):
            @blk.sync
            def _(sy, tt=tt, mkf=m_ln5_done):
                W.wge(sy, mkf)
                W.inc(sy.dma_start(xt5[:, :], x_own[128 * tt:128 * (tt + 1), :]),
                      "sH", 16)
            m_ld5 = W.snap()

            m_ln5_done = emit_ln_tile(xt5, st5, hring[:, 0:D], xnb5, False,
                                      [m_ld5], [m_xnb5_free], save_tt=tt)

            @blk.sync
            def _(sy, tt=tt, mk=m_ln5_done):
                W.wge(sy, mk)
                W.inc(sy.dma_start(
                    bass.AP(xnf, 128 * tt, [[FC * OWN, 128], [OWN, FC], [1, 128]]),
                    xnb5.rearrange("t (j p) -> t j p", p=128),
                    transpose=True), "sH", 16)
            m_xnb5_free = W.snap()
        m_xnf = W.snap()

        if cfg.debug:
            @blk.gpsimd
            def _(g):
                W.w(g)
                W.inc(g.dma_start(dbg_xnf.ap(), xnf.ap()), "sL", 16)

        m_w_free = [m_P5start, m_P5start, m_P5start]
        m_wT_free = [m_P5start, m_P5start]
        m_gu_free = [m_P5start, m_P5start]
        m_proj_done = m_P5start
        m_sqh_free = m_P5start
        m_wp_free = [m_P5start, m_P5start]
        m_wpT_free = m_P5start
        m_gacc_free = [m_P5start, m_P5start]

        HCH = cfg.HCH
        m_wTT_of = {}
        m_h_super = {}

        def stage_L(k):
            if not (0 <= k < HCH):
                return
            r3 = k % 3

            @blk.gpsimd
            def _(g, k=k, r3=r3, mkf=m_w_free[r3]):
                W.wge(g, mkf)
                W.inc(g.dma_start(wg_n[r3][:, :], w_gu[128 * k:128 * (k + 1), :]),
                      "sL", 16)
                W.inc(g.dma_start(wu_n[r3][:, :],
                                  w_gu[HF + 128 * k: HF + 128 * (k + 1), :]), "sL", 16)
            m_ld_of[k] = W.snap()

        def stage_Q(k):
            if not (0 <= k < HCH):
                return
            r3 = k % 3
            emit_quant(wg_n[r3], wg_n[r3], D, [m_ld_of[k]])
            m_q_of[k] = emit_quant(wu_n[r3], wu_n[r3], D, [m_ld_of[k]])

        def stage_T(k):
            if not (0 <= k < HCH):
                return
            r3 = k % 3
            r2 = k % 2

            @blk.sync
            def _(sy, r3=r3, r2=r2, mk=m_q_of[k], mkT=m_wT_free[r2]):
                W.wge(sy, mk, mkT)
                W.inc(sy.dma_start(
                    bass.AP(wgT[r2], 0, [[FC * 128, 128], [128, FC], [1, 128]]),
                    wg_n[r3].ap().rearrange("t (j p) -> t j p", p=128), transpose=True),
                      "sH", 16)
                W.inc(sy.dma_start(
                    bass.AP(wuT[r2], 0, [[FC * 128, 128], [128, FC], [1, 128]]),
                    wu_n[r3].ap().rearrange("t (j p) -> t j p", p=128), transpose=True),
                      "sH", 16)
            m_wTT_of[k] = W.snap()
            m_w_free[r3] = m_wTT_of[k]

        def stage_Ma(k):
            b = k % 8
            r2 = k % 2
            for hf in range(NHALF):
                gi = (2 * b + hf) % 2

                @blk.tensor
                def _(t, hf=hf, r2=r2, gi=gi, mk=m_wTT_of[k], mkf=m_gu_free[gi]):
                    W.wge(t, mk, mkf)
                    for fc in range(FC):
                        W.inc(t.matmul(psG[gi][0:128, 0:THALF],
                                       wgT[r2][:, fc * 128:(fc + 1) * 128],
                                       bass.AP(xnf, fc * OWN + THALF * hf,
                                               [[FC * OWN, 128], [1, THALF]]),
                                       start=(fc == 0), stop=(fc == FC - 1)), "sT", 1)
                    for fc in range(FC):
                        W.inc(t.matmul(psU[gi][0:128, 0:THALF],
                                       wuT[r2][:, fc * 128:(fc + 1) * 128],
                                       bass.AP(xnf, fc * OWN + THALF * hf,
                                               [[FC * OWN, 128], [1, THALF]]),
                                       start=(fc == 0), stop=(fc == FC - 1)), "sT", 1)
                m_mm = W.snap()
                m_wT_free[r2] = m_mm

                @blk.scalar
                def _(sc, gi=gi, mk=m_mm):
                    c = _Chained(W, sc, "sA")
                    W.wge(sc, mk)
                    c.activation(silscr[gi][:, :], psG[gi][0:128, 0:THALF], AF.Silu)
                m_sil_of[(k, hf)] = W.snap()

        def stage_Mb(k):
            nonlocal m_sqh_free
            b = k % 8
            for hf in range(NHALF):
                gi = (2 * b + hf) % 2

                @blk.vector
                def _(v, b=b, hf=hf, gi=gi, mk=m_sil_of[(k, hf)],
                      mkp=m_proj_done, mks=m_sqh_free):
                    c = _Chained(W, v, "sV")
                    W.wge(v, mk, mkp, mks)
                    hs = hring[:, b * OWN + THALF * hf: b * OWN + THALF * (hf + 1)]
                    c.tensor_tensor(hs, silscr[gi][:, :], psU[gi][0:128, 0:THALF],
                                    Alu.mult)
                    c.tensor_tensor(sqh[:, :], hs, hs, Alu.mult)
                m_gu_free[gi] = W.snap()
                m_h = W.snap()
                m_h_super[k // 8] = m_h

                @blk.tensor
                def _(t, k=k, hf=hf, mk=m_h):
                    W.wge(t, mk)
                    for ttl in range(THALF // 128):
                        tt = (THALF // 128) * hf + ttl
                        W.inc(t.matmul(psT5[0:128, tt:tt + 1],
                                       sqh[:, 128 * ttl:128 * (ttl + 1)],
                                       ones_b[:, 0:1],
                                       start=(k == 0 and hf == 0 and ttl == 0),
                                       stop=(k == HCH - 1 and hf == NHALF - 1
                                             and ttl == THALF // 128 - 1),
                                       skip_group_check=True), "sT", 1)
                m_sqh_free = W.snap()

        m_wpTr = {}

        def emit_wp(ks, i):
            r = i % 2

            @blk.gpsimd
            def _(g, ks=ks, i=i, r=r, mkf=m_wp_free[r]):
                W.wge(g, mkf)
                W.inc(g.dma_start(wp_n[r][:, :],
                                  w_proj[128 * i:128 * (i + 1),
                                         1024 * ks:1024 * (ks + 1)]), "sL", 16)
            m_ld = W.snap()

            m_q = emit_quant(wp_n[r], wp_n[r], 1024, [m_ld])

            @blk.sync
            def _(sy, i=i, r=r, mk=m_q, mkT=m_wpT_free):
                W.wge(sy, mk, mkT)
                W.inc(sy.dma_start(
                    bass.AP(wpT, 128 * i, [[8 * D, 128], [D, 8], [1, 128]]),
                    wp_n[r].ap()[:, 0:1024].rearrange("t (j p) -> t j p", p=128),
                    transpose=True), "sH", 16)
            m_wp_free[r] = W.snap()
            m_wpTr[(ks, i)] = m_wp_free[r]

        def super_tail(ks):
            nonlocal m_proj_done, m_wpT_free
            for dc in range(NDC):
                for tt in range(OT):
                    pi = (dc * OT + tt) % 2
                    pb = psP[pi]

                    @blk.tensor
                    def _(t, tt=tt, dc=dc, pb=pb, mk=m_wpTr[(ks, 4 * dc + 3)],
                          mkh=m_h_super[ks], mkf=m_gacc_free[pi]):
                        W.wge(t, mk, mkh, mkf)
                        for b in range(8):
                            W.inc(t.matmul(pb[0:128, 0:512],
                                           hring[:, b * OWN + 128 * tt: b * OWN + 128 * (tt + 1)],
                                           wpT[:, b * D + 512 * dc: b * D + 512 * (dc + 1)],
                                           start=(b == 0), stop=(b == 7)), "sT", 1)
                    m_mm = W.snap()

                    @blk.vector
                    def _(v, ks=ks, tt=tt, dc=dc, pb=pb, mk=m_mm):
                        c = _Chained(W, v, "sV")
                        W.wge(v, mk)
                        dst = gacc[:, tt * D + 512 * dc: tt * D + 512 * (dc + 1)]
                        if ks == 0:
                            c.tensor_copy(dst, pb[0:128, 0:512])
                        else:
                            c.tensor_tensor(dst, dst, pb[0:128, 0:512], Alu.add)
                    m_gacc_free[pi] = W.snap()
            m_proj_done = W.snap()
            m_wpT_free = m_proj_done

        m_ld_of = {}
        m_q_of = {}
        m_sil_of = {}
        stage_L(0)
        stage_L(1)
        stage_Q(0)
        stage_T(0)
        NWP = D // 128
        wpcum = [min(NWP, (NWP * x + 5) // 6) for x in range(7)]
        for k in range(HCH + 1):
            stage_L(k + 2)
            stage_Q(k + 1)
            stage_T(k + 1)
            if k >= 1:
                stage_Mb(k - 1)
                if (k - 1) % 8 == 7:
                    super_tail((k - 1) // 8)
            c = k % 8
            if k < HCH and c >= 2:
                for i in range(wpcum[c - 2], wpcum[c - 1]):
                    emit_wp(k // 8, i)
            if k < HCH:
                stage_Ma(k)


        @blk.scalar
        def _(sc):
            c = _Chained(W, sc, "sA")
            W.w(sc)
            c.activation(muI[:, 0:OT], psT5[0:128, 0:OT], AF.Sqrt,
                         scale=cAP(C_INVHF), bias=cAP(C_EPSF32))

        @blk.vector
        def _(v):
            c = _Chained(W, v, "sV")
            W.w(v)
            c.reciprocal(muI[:, 0:OT], muI[:, 0:OT])
        m_muI = W.snap()

        if cfg.debug:
            @blk.vector
            def _(v):
                c = _Chained(W, v, "sV")
                W.w(v)
                c.tensor_copy(xt5[:, 0:OT], psT5[0:128, 0:OT])

            @blk.gpsimd
            def _(g):
                W.w(g)
                W.inc(g.dma_start(dbg_gacc.ap(), gacc.ap()), "sL", 16)
                W.inc(g.dma_start(dbg_muI.ap()[:, 0:OT], muI[:, 0:OT]), "sL", 16)
                W.inc(g.dma_start(dbg_psT.ap()[:, 0:OT], xt5[:, 0:OT]), "sL", 16)
                W.inc(g.dma_start(dbg_sqh.ap(), sqh[:, :]), "sL", 16)

        m_P6start = W.snap()
        p5W.close()
        p6 = ExitStack()
        xt6 = [self.sb(p6, f"xt6_{i}", [128, D], F32, side="right") for i in range(2)]
        yh6 = [self.sb(p6, f"yh6_{i}", [128, D], F32, side="right") for i in range(2)]
        gsil = [self.sb(p6, f"gsil{i}", [128, D], F32, side="right") for i in range(2)]
        xnh = self.sb(p6, "xnh", [128, D], F32, side="right")
        outt = [self.sb(p6, f"outt{i}", [128, D], F32, side="right") for i in range(2)]
        m_cmb_done = [W.snap(), W.snap()]
        m_out_free = [W.snap(), W.snap()]

        for tt in range(OT):
            r = tt % 2

            @blk.sync
            def _(sy, tt=tt, r=r, mkf=m_cmb_done[r]):
                W.wge(sy, mkf)
                W.inc(sy.dma_start(xt6[r][:, :], x_own[128 * tt:128 * (tt + 1), :]),
                      "sH", 16)
                W.inc(sy.dma_start(yh6[r][:, :], y_spill[128 * tt:128 * (tt + 1), :]),
                      "sH", 16)
            m_ld = W.snap()

            @blk.scalar
            def _(sc, tt=tt, r=r, mk=m_muI, mkf=m_cmb_done[r]):
                c = _Chained(W, sc, "sA")
                W.wge(sc, mk, mkf)
                c.activation(gsil[r][:, :], gacc[:, tt * D:(tt + 1) * D], AF.Silu,
                             scale=muI[:, tt:tt + 1])
            m_gs = W.snap()

            @blk.vector
            def _(v, tt=tt, r=r, mk=m_ld, mkg=m_gs, mko=m_out_free[r]):
                c = _Chained(W, v, "sV")
                W.wge(v, mk, mkg, mko)
                c.tensor_scalar(xnh[:, :], xt6[r][:, :],
                                muR[:, 2 * tt:2 * tt + 1], muR[:, 2 * tt + 1:2 * tt + 2],
                                Alu.subtract, Alu.mult)
                c.tensor_tensor(xnh[:, :], xnh[:, :], yh6[r][:, :], Alu.add)
                c.tensor_tensor(xnh[:, :], xnh[:, :], gsil[r][:, :], Alu.mult)
                c.tensor_tensor(outt[r][:, :], xnh[:, :], xt6[r][:, :], Alu.add)
            m_cmb_done[r] = W.snap()
            m_cmb = W.snap()

            @blk.sync
            def _(sy, tt=tt, r=r, mk=m_cmb):
                W.wge(sy, mk)
                W.inc(sy.dma_start(out_own[128 * tt:128 * (tt + 1), :], outt[r][:, :]),
                      "sH", 16)
            m_out_free[r] = W.snap()

        @blk.sync
        def _(sy):
            W.w(sy)

        phase_barrier()
        p6.close()
        p5.close()
        p5X.close()
        self.es.close()
        return self.nc


def _tables(S):
    invf = 1.0 / (10000.0 ** (np.arange(0, 64, 2, dtype=np.float64) / 64.0))
    ang = np.arange(S, dtype=np.float64)[:, None] * invf[None, :]
    return np.cos(ang).astype(np.float32), np.sin(ang).astype(np.float32)


def make_inputs_for_core(cfg: Cfg, core, x, w_in, w_out, rms_w, w_gu, w_proj):
    import ml_dtypes
    S, D, QT, SEG = cfg.S, cfg.D, cfg.QT, cfg.SEG
    seq = core // 4
    cp = core % 4
    own_idx = np.concatenate([SEG * j + QT * cp + np.arange(QT) for j in range(cfg.NSLOT)])
    kc, ks_ = _tables(S)
    ql = np.arange(QT)[:, None]
    kk = np.arange(SEG)[None, :]
    tri = np.where(kk <= QT * cp + ql, 0.0, MASK_NEG).astype(np.float32)
    consts = np.zeros((128, NCONST), np.float32)
    consts[:, C_INV64] = 1.0 / 64
    consts[:, C_P125] = 0.125
    consts[:, C_EPS_LN] = 1e-5
    consts[:, C_EPS_RMS] = 1e-6
    consts[:, C_INVHF] = 1.0 / cfg.HF
    consts[:, C_EPSF32] = 1.1920929e-7
    consts[:, C_NEG1] = -1.0
    return {
        "x_seq": np.ascontiguousarray(x[seq]),
        "x_own": np.ascontiguousarray(x[seq][own_idx]),
        "w_in": w_in, "w_out": w_out, "w_gu": w_gu, "w_proj": w_proj,
        "rms_rep": np.ascontiguousarray(np.broadcast_to(rms_w[None, :], (128, D))).astype(np.float32),
        "kcos": kc, "ksin": ks_,
        "qcos": np.ascontiguousarray(kc[own_idx]), "qsin": np.ascontiguousarray(ks_[own_idx]),
        "trimaskb": tri.astype(ml_dtypes.bfloat16),
        "identb": np.eye(128, dtype=np.float32).astype(ml_dtypes.bfloat16),
        "consts": consts,
        "ones_in": np.ones((128, 64), np.float32),
        "ident_in": np.eye(128, dtype=np.float32),
        "onesb_in": np.ones((128, 1), np.float32).astype(ml_dtypes.bfloat16),
    }, own_idx, seq


def kernel(L0_attn_x, L0_attn_w_in, L0_attn_w_out, L0_attn_rms_w,
           L0_ffn_w_gate_up, L0_ffn_w_proj):
    cfg = Cfg()
    x = np.asarray(L0_attn_x)
    nc = B(cfg).build()
    in_maps = []
    metas = []
    for core in range(8):
        m, own_idx, seq = make_inputs_for_core(
            cfg, core, x, np.asarray(L0_attn_w_in), np.asarray(L0_attn_w_out),
            np.asarray(L0_attn_rms_w), np.asarray(L0_ffn_w_gate_up),
            np.asarray(L0_ffn_w_proj))
        in_maps.append(m)
        metas.append((own_idx, seq))
    res = run_bass_kernel_spmd(nc, in_maps, core_ids=list(range(8)))
    out = np.empty_like(x)
    for core in range(8):
        own_idx, seq = metas[core]
        out[seq][own_idx] = np.asarray(res.results[core]["out_own"])
    return out


# version 15 (final)
# speedup vs baseline: 1.0378x; 1.0378x over previous
import numpy as np
from contextlib import ExitStack

import concourse.bass as bass
import concourse.mybir as mybir
from concourse.bass_utils import run_bass_kernel_spmd

F32 = mybir.dt.float32
BF16 = mybir.dt.bfloat16
AF = mybir.ActivationFunctionType
Alu = mybir.AluOpType
AX = mybir.AxisListType.X

MASK_NEG = -1.0e9
THR_BF16 = 0.501953125
POOL_QUANT = False

C_INV64, C_P125, C_EPS_LN, C_EPS_RMS, C_INVHF, C_EPSF32, C_NEG1 = range(7)
NCONST = 8


class Cfg:
    def __init__(self, S=4096, D=2048, NG=8, HPG=4, HF=None, debug=0):
        self.S, self.D, self.NG, self.HPG = S, D, NG, HPG
        self.debug = debug
        self.DH = 64
        self.NH = NG * HPG
        self.HF = HF if HF is not None else 4 * D
        self.FC = D // 128
        self.QT = S // 32
        self.SEG = S // 8
        self.NSLOT = 8
        self.OWN = self.NSLOT * self.QT
        self.QCOLS = self.NH * self.DH
        self.KCOLS = NG * self.DH
        self.CC = self.QCOLS // 128
        self.NPAIR = NG // 2
        self.NT = S // 128
        self.OT = self.OWN // 128
        self.HCH = self.HF // 128


_CHAIN_OPS = ("tensor_reduce", "tensor_scalar", "tensor_tensor", "tensor_copy",
              "reciprocal", "memset", "scalar_tensor_tensor", "activation",
              "tensor_add", "tensor_sub", "tensor_mul")


class _AlreadyChained:
    pass


class _Chained:

    def __init__(self, em, eng, sem_name):
        self._em, self._eng, self._s = em, eng, sem_name

    def __getattr__(self, name):
        f = getattr(self._eng, name)
        if name not in _CHAIN_OPS:
            return f

        def g(*a, **k):
            ins = f(*a, **k)
            self._em.inc(ins, self._s, 1)
            self._eng.drain()
            return _AlreadyChained()
        return g


class B:

    def __init__(self, cfg: Cfg):
        self.cfg = cfg
        self.nc = bass.Bass("TRN2", target_bir_lowering=False)
        self.nc.detect_race_conditions = False
        self.es = ExitStack()
        self.c = {}
        self.sem = {}
        for s in ("sL", "sH", "sT", "sA", "sV", "sG"):
            self.sem[s] = self.es.enter_context(self.nc.semaphore(s))
            self.c[s] = 0

    def inc(self, instr, name, amt):
        if isinstance(instr, _AlreadyChained):
            return instr
        self.c[name] += amt
        return instr.then_inc(self.sem[name], amt)

    def snap(self):
        return dict(self.c)

    def wge(self, eng, *snaps):
        if getattr(self, "serial", False):
            return self.w(eng)
        m = {}
        for s in snaps:
            if not s:
                continue
            for k, v in s.items():
                if v > m.get(k, 0):
                    m[k] = v
        for k, v in m.items():
            if v > 0:
                eng.wait_ge(self.sem[k], v)

    def w(self, eng, *names):
        for n in self.c:
            if self.c[n] > 0:
                eng.wait_ge(self.sem[n], self.c[n])

    def sb(self, st, name, shape, dt, side=None):
        if side is None:
            return st.enter_context(self.nc.sbuf_tensor(name, shape, dt))
        return st.enter_context(self.nc.sbuf_tensor(name, shape, dt, side=side))

    def ps(self, st, name, shape, dt=F32):
        return st.enter_context(self.nc.psum_tensor(name, shape, dt))

    def build(self):
        cfg, nc = self.cfg, self.nc
        S, D, NG, HPG, DH = cfg.S, cfg.D, cfg.NG, cfg.HPG, cfg.DH
        HF, FC, QT, SEG = cfg.HF, cfg.FC, cfg.QT, cfg.SEG
        QCOLS, KCOLS, CC, NPAIR = cfg.QCOLS, cfg.KCOLS, cfg.CC, cfg.NPAIR
        OWN, NT, OT, NSLOT = cfg.OWN, cfg.NT, cfg.OT, cfg.NSLOT
        SEGB = SEG // 128

        dt_ = nc.dram_tensor
        x_seq = dt_("x_seq", [S, D], F32, kind="ExternalInput")
        x_own = dt_("x_own", [OWN, D], F32, kind="ExternalInput")
        w_in = dt_("w_in", [QCOLS + 2 * KCOLS, D], F32, kind="ExternalInput")
        w_out = dt_("w_out", [D, QCOLS], F32, kind="ExternalInput")
        w_gu = dt_("w_gu", [2 * HF, D], F32, kind="ExternalInput")
        w_proj = dt_("w_proj", [D, HF], F32, kind="ExternalInput")
        rms_rep = dt_("rms_rep", [128, D], F32, kind="ExternalInput")
        kcos_d = dt_("kcos", [S, 32], F32, kind="ExternalInput")
        ksin_d = dt_("ksin", [S, 32], F32, kind="ExternalInput")
        qcos_d = dt_("qcos", [OWN, 32], F32, kind="ExternalInput")
        qsin_d = dt_("qsin", [OWN, 32], F32, kind="ExternalInput")
        tmaskb_d = dt_("trimaskb", [QT, SEG], BF16, kind="ExternalInput")
        identb_d = dt_("identb", [128, 128], BF16, kind="ExternalInput")
        consts_d = dt_("consts", [128, NCONST], F32, kind="ExternalInput")
        ones_d = dt_("ones_in", [128, 64], F32, kind="ExternalInput")
        ident_d = dt_("ident_in", [128, 128], F32, kind="ExternalInput")
        onesb_d = dt_("onesb_in", [128, 1], BF16, kind="ExternalInput")
        y_spill = dt_("y_spill", [OWN, D], F32)
        out_own = dt_("out_own", [OWN, D], F32, kind="ExternalOutput")
        if cfg.debug:
            dbg_K2 = dt_("dbg_K2", [128, NPAIR * S], F32, kind="ExternalOutput")
            dbg_Vt = dt_("dbg_Vt", [128, NG * (S // 128) * (DH + 1)], F32, kind="ExternalOutput")
            dbg_Qp = dt_("dbg_Qp", [128, NPAIR * HPG * OWN], F32, kind="ExternalOutput")
            dbg_ofm = dt_("dbg_ofm", [128, CC * OWN], F32, kind="ExternalOutput")
            dbg_xnf = dt_("dbg_xnf", [128, FC * OWN], F32, kind="ExternalOutput")
            dbg_gacc = dt_("dbg_gacc", [128, OT * D], F32, kind="ExternalOutput")
            dbg_muI = dt_("dbg_muI", [128, max(OT, 8)], F32, kind="ExternalOutput")
            dbg_psT = dt_("dbg_psT", [128, 512], F32, kind="ExternalOutput")
            dbg_st = dt_("dbg_st", [128, 8 + NCONST], F32, kind="ExternalOutput")
            dbg_xrb = dt_("dbg_xrb", [128, D], F32, kind="ExternalOutput")
            dbg_kcos = dt_("dbg_kcos", [128, NT * 32], F32, kind="ExternalOutput")
            dbg_wkvT = dt_("dbg_wkvT", [128, FC * 2 * KCOLS], F32, kind="ExternalOutput")
            dbg_sqh = dt_("dbg_sqh", [128, min(512, OWN)], F32, kind="ExternalOutput")
            dbg_s1 = dt_("dbg_s1", [QT, 2 * HPG * 8], F32, kind="ExternalOutput")
            dbg_be = dt_("dbg_be", [QT, 2 * HPG], F32, kind="ExternalOutput")
            dbg_Pb0 = dt_("dbg_Pb0", [128, SEG], F32, kind="ExternalOutput")
            dbg_s2 = dt_("dbg_s2", [QT, 2 * HPG * 8], F32, kind="ExternalOutput")
            dbg_rr = dt_("dbg_rr", [QT, 2 * HPG], F32, kind="ExternalOutput")
            dbg_ost = dt_("dbg_ost", [128, HPG * QT], F32, kind="ExternalOutput")
            dbg_bc = dt_("dbg_bc", [128, HPG * QT], F32, kind="ExternalOutput")

        es = self.es
        consts = self.sb(es, "consts_sb", [128, NCONST], F32)
        ones_f = self.sb(es, "ones_f", [128, 64], F32)
        ident_f = self.sb(es, "ident_f", [128, 128], F32)
        ones_b = self.sb(es, "ones_b", [128, 1], BF16)
        tmaskb = self.sb(es, "tmaskb_sb", [QT, SEG], BF16)
        identb = self.sb(es, "identb_sb", [128, 128], BF16)
        muR = self.sb(es, "muR", [128, OT * 2], F32)

        patt = ExitStack()
        K2 = self.sb(patt, "K2", [128, NPAIR * S], BF16)
        Vt = self.sb(patt, "Vt", [128, NG * (S // 128) * (DH + 1)], BF16)
        pQp = ExitStack()

        VST = (S // 128) * (DH + 1)

        blk = es.enter_context(nc.Block())

        def cAP(col, p=128):
            return consts[0:p, col:col + 1]

        W = self

        def preload(func):
            @blk.scalar
            def _(sc):
                sc.preload_act_table(func)

        def phase_barrier():
            for eng in (blk.tensor, blk.vector, blk.scalar, blk.gpsimd, blk.sync):
                @eng
                def _(e):
                    W.w(e)

        p0 = ExitStack()
        kcos = self.sb(p0, "kcos_sb", [128, NT * 32], F32, side="right")
        ksin = self.sb(p0, "ksin_sb", [128, NT * 32], F32, side="right")
        qcos = self.sb(p0, "qcos_sb", [128, OT * 32], F32, side="right")
        qsin = self.sb(p0, "qsin_sb", [128, OT * 32], F32, side="right")
        rmsw = self.sb(p0, "rmsw_sb", [128, D], F32, side="right")


        @blk.sync
        def _(sy):
            for dst, src in ((consts, consts_d), (ones_f, ones_d), (ident_f, ident_d),
                             (tmaskb, tmaskb_d), (identb, identb_d),
                             (rmsw, rms_rep), (ones_b, onesb_d)):
                W.inc(sy.dma_start(dst.ap(), src.ap()), "sH", 16)
            for dst, src in ((kcos, kcos_d), (ksin, ksin_d), (qcos, qcos_d), (qsin, qsin_d)):
                W.inc(sy.dma_start(
                    dst.ap().rearrange("p (n c) -> p n c", c=32),
                    src.ap().rearrange("(n p) c -> p n c", p=128)), "sH", 16)

        @blk.gpsimd
        def _(g):
            for grp in range(NG):
                W.inc(g.memset(
                    bass.AP(Vt, grp * VST + DH,
                            [[Vt.shape[1], 128], [DH + 1, S // 128], [1, 1]]), 1.0), "sG", 1)
        m_p0 = W.snap()

        p12 = ExitStack()
        wstg = [self.sb(p12, f"wstg{i}", [128, D], BF16, side="right") for i in range(2)]
        xt = [self.sb(p12, f"xt{i}", [128, D], F32, side="right") for i in range(2)]
        stt = [self.sb(p12, f"stt{i}", [128, 8], F32, side="right") for i in range(2)]
        sqjunk = self.sb(p12, "sqjunk", [128, D], BF16, side="right")
        xrs = self.sb(p12, "xrs", [128, D], F32, side="right")
        xrb = [self.sb(p12, f"xrb{i}", [128, D], BF16, side="right") for i in range(2)]
        TB = 512
        xrf = self.sb(p12, "xrf", [128, FC * TB], BF16, side="right")
        ropeA = self.sb(p12, "ropeA", [128, (max(KCOLS, 512) // DH) * 32], F32, side="right")
        ropeB = self.sb(p12, "ropeB", [128, (max(KCOLS, 512) // DH) * 32], F32, side="right")
        pmm = [self.ps(p12, f"pmm{i}", [128, 512], F32) for i in range(2)]

        def emit_ln_tile(xtb, stb, junkb, dst_bf, with_rms, wait0, wait_dst=None,
                         save_tt=None):
            st = stb

            @blk.scalar
            def _(sc):
                c = _Chained(W, sc, "sA")
                W.wge(sc, *wait0)
                c.activation(junkb[:, :], xtb[:, :], AF.Square, accum_out=st[:, 4:5])
            m_sq = W.snap()

            @blk.vector
            def _(v):
                c = _Chained(W, v, "sV")
                W.wge(v, *wait0)
                c.tensor_reduce(st[:, 3:4], xtb[:, :], AX, Alu.add)
                c.tensor_scalar(st[:, 0:1], st[:, 3:4], 1.0 / D, None, Alu.mult)
                W.wge(v, m_sq)
                c.tensor_scalar(st[:, 5:6], st[:, 4:5], 1.0 / D, None, Alu.mult)
                c.tensor_scalar(st[:, 6:7], st[:, 0:1], st[:, 0:1], None, Alu.mult)
                c.tensor_tensor(st[:, 6:7], st[:, 5:6], st[:, 6:7], Alu.subtract)
                if with_rms:
                    c.tensor_scalar(st[:, 5:6], st[:, 6:7], 1.0 + 1e-6, 1e-11,
                                    Alu.mult, Alu.add)
                    c.tensor_scalar(st[:, 6:7], st[:, 6:7], 1.0, 1e-5, Alu.mult, Alu.add)
                else:
                    c.tensor_scalar(st[:, 5:6], st[:, 6:7], 1.0, 1e-5, Alu.mult, Alu.add)
            m_v1 = W.snap()

            @blk.scalar
            def _(sc):
                c = _Chained(W, sc, "sA")
                W.wge(sc, m_v1)
                n = 2 if with_rms else 1
                c.activation(st[:, 5:5 + n], st[:, 5:5 + n], AF.Sqrt)
            m_a2 = W.snap()

            @blk.vector
            def _(v):
                c = _Chained(W, v, "sV")
                W.wge(v, m_a2)
                if wait_dst:
                    W.wge(v, *wait_dst)
                n = 2 if with_rms else 1
                c.reciprocal(st[:, 1:1 + n], st[:, 5:5 + n])
                if with_rms:
                    c.tensor_scalar(xrs[:, :], xtb[:, :], st[:, 0:1], st[:, 1:2],
                                    Alu.subtract, Alu.mult)
                    c.tensor_tensor(dst_bf[:, :], xrs[:, :], rmsw[:, :], Alu.mult)
                else:
                    c.tensor_scalar(dst_bf[:, :], xtb[:, :], st[:, 0:1], st[:, 1:2],
                                    Alu.subtract, Alu.mult)
                if save_tt is not None:
                    c.tensor_copy(muR[:, 2 * save_tt:2 * save_tt + 1], st[:, 0:1])
                    c.tensor_copy(muR[:, 2 * save_tt + 1:2 * save_tt + 2], st[:, 1:2])
            return W.snap()

        def emit_rope(pb, cos_sb, sin_sb, ti, ncols, out_bf, wait0):
            ngc = ncols // DH

            @blk.vector
            def _(v):
                c = _Chained(W, v, "sV")
                W.wge(v, *wait0)
                ct = cos_sb[:, 32 * ti:32 * (ti + 1)].unsqueeze(1).broadcast_to([128, ngc, 32])
                st_ = sin_sb[:, 32 * ti:32 * (ti + 1)].unsqueeze(1).broadcast_to([128, ngc, 32])
                x1 = bass.AP(pb, 0, [[512, 128], [DH, ngc], [1, 32]])
                x2 = bass.AP(pb, 32, [[512, 128], [DH, ngc], [1, 32]])
                a = ropeA.ap().rearrange("p (g c) -> p g c", c=32)[:, 0:ngc, :]
                b = ropeB.ap().rearrange("p (g c) -> p g c", c=32)[:, 0:ngc, :]
                o = out_bf.ap()[:, 0:ncols].rearrange("p (g two c) -> p g two c", two=2, c=32)
                c.tensor_tensor(a, x1, ct, Alu.mult)
                c.tensor_tensor(b, x2, st_, Alu.mult)
                c.tensor_tensor(o[:, :, 0, :], a, b, Alu.subtract)
                c.tensor_tensor(a, x2, ct, Alu.mult)
                c.tensor_tensor(b, x1, st_, Alu.mult)
                c.tensor_tensor(o[:, :, 1, :], a, b, Alu.add)
            return W.snap()

        p1 = ExitStack()
        NKV = 2 * KCOLS
        wkvT = self.sb(p1, "wkvT", [128, FC * NKV], BF16, side="right")
        ktm = [self.sb(p1, f"ktm{i}", [128, KCOLS], BF16, side="right") for i in range(2)]

        m_wstg_free = [m_p0, m_p0]
        m_wstg_ld = [None, None]
        for ch in range(NKV // 128):
            r = ch % 2

            @blk.gpsimd
            def _(g, ch=ch, r=r, mkf=m_wstg_free[r]):
                W.wge(g, mkf)
                W.inc(g.dma_start(wstg[r][:, :],
                                  w_in[QCOLS + 128 * ch: QCOLS + 128 * (ch + 1), :]), "sL", 16)
            m_wstg_ld[r] = W.snap()

            @blk.sync
            def _(sy, ch=ch, r=r, mk=m_wstg_ld[r]):
                W.wge(sy, mk)
                W.inc(sy.dma_start(
                    bass.AP(wkvT, 128 * ch, [[FC * NKV, 128], [NKV, FC], [1, 128]]),
                    wstg[r].ap().rearrange("t (j p) -> t j p", p=128),
                    transpose=True), "sH", 16)
            m_wstg_free[r] = W.snap()
        m_wkvT = W.snap()

        NKVC = (NKV + 511) // 512
        TPB = TB // 128
        m_ln_done = [m_p0, m_p0]
        m_xrb_free = [m_p0, m_p0]
        m_xld = [None, None]
        m_mm_free = [m_p0, m_p0]
        m_ktm_free = [m_p0, m_p0]
        m_xrf_free = m_p0

        for tb in range(S // TB):
            for tt in range(TPB):
                ti = tb * TPB + tt
                r = ti % 2

                @blk.sync
                def _(sy, ti=ti, r=r, mkf=m_ln_done[r]):
                    W.wge(sy, mkf)
                    W.inc(sy.dma_start(xt[r][:, :], x_seq[128 * ti:128 * (ti + 1), :]),
                          "sH", 16)
                m_xld[r] = W.snap()

                m_ln_done[r] = emit_ln_tile(xt[r], stt[r], sqjunk, xrb[r], True,
                                            [m_xld[r]], [m_xrb_free[r]])

                @blk.sync
                def _(sy, tt=tt, r=r, mk=m_ln_done[r], mkx=m_xrf_free):
                    W.wge(sy, mk, mkx)
                    W.inc(sy.dma_start(
                        bass.AP(xrf, 128 * tt, [[FC * TB, 128], [TB, FC], [1, 128]]),
                        xrb[r].ap().rearrange("t (j p) -> t j p", p=128),
                        transpose=True), "sH", 16)
                m_xrb_free[r] = W.snap()
            m_batchT = W.snap()

            for kc in range(NKVC):
                ncols = min(512, NKV - 512 * kc)
                for tt in range(TPB):
                    ti = tb * TPB + tt
                    p2i = (kc * TPB + tt) % 2
                    pb = pmm[p2i]

                    @blk.tensor
                    def _(t, kc=kc, tt=tt, pb=pb, ncols=ncols,
                          mkb=m_batchT, mkf=m_mm_free[p2i]):
                        W.wge(t, mkb, m_wkvT, mkf)
                        for fc in range(FC):
                            W.inc(t.matmul(pb[0:128, 0:ncols],
                                           bass.AP(xrf, fc * TB + 128 * tt,
                                                   [[FC * TB, 128], [1, 128]]),
                                           bass.AP(wkvT, fc * NKV + 512 * kc,
                                                   [[FC * NKV, 128], [1, ncols]]),
                                           start=(fc == 0), stop=(fc == FC - 1)), "sT", 1)
                    m_mm = W.snap()

                    if 512 * kc < KCOLS:
                        kr = ti % 2
                        emit_rope(pb, kcos, ksin, ti, ncols, ktm[kr],
                                  [m_mm, m_ktm_free[kr]])
                        m_mm_free[p2i] = W.snap()
                        m_rope = W.snap()

                        @blk.sync
                        def _(sy, ti=ti, kr=kr, ncols=ncols, mk=m_rope):
                            W.wge(sy, mk)
                            W.inc(sy.dma_start(
                                bass.AP(K2, 128 * ti, [[NPAIR * S, 128], [S, ncols // 128], [1, 128]]),
                                ktm[kr].ap()[:, 0:ncols].rearrange("t (j p) -> t j p", p=128),
                                transpose=True), "sH", 16)
                        m_ktm_free[kr] = W.snap()
                    else:
                        @blk.vector
                        def _(v, pb=pb, ti=ti, ncols=ncols, kc=kc, mk=m_mm):
                            c = _Chained(W, v, "sV")
                            W.wge(v, mk)
                            ngc = ncols // DH
                            g0 = (512 * kc - KCOLS) // DH
                            src = bass.AP(pb, 0, [[512, 128], [DH, ngc], [1, DH]])
                            dst = bass.AP(Vt, g0 * VST + ti * (DH + 1),
                                          [[Vt.shape[1], 128], [VST, ngc], [1, DH]])
                            c.tensor_copy(dst, src)
                        m_mm_free[p2i] = W.snap()
            m_xrf_free = W.snap()

        p1.close()
        p2 = ExitStack()
        Qp = self.sb(pQp, "Qp", [128, NPAIR * HPG * OWN], BF16)
        QPW = Qp.shape[1]
        wqT = self.sb(p2, "wqT", [128, FC * 512], BF16, side="right")
        qtm1 = self.sb(p2, "qtm", [128, 512], BF16, side="right")
        qtm = [qtm1, qtm1]
        qstg1 = self.sb(p2, "qstg", [128, 512], BF16, side="right")
        qstg = [qstg1, qstg1]
        OH = max(1, OT // 2)
        m_qtm_free = [W.snap(), W.snap()]
        m_qstg_free = [W.snap(), W.snap()]
        m_wqT_free = W.snap()

        for half in range((OT + OH - 1) // OH):
            tiles = list(range(half * OH, min(OT, (half + 1) * OH)))
            for tt in tiles:
                r = tt % 2

                @blk.sync
                def _(sy, tt=tt, r=r, mkf=m_ln_done[r]):
                    W.wge(sy, mkf)
                    W.inc(sy.dma_start(xt[r][:, :], x_own[128 * tt:128 * (tt + 1), :]),
                          "sH", 16)
                m_xld[r] = W.snap()

                m_ln_done[r] = emit_ln_tile(xt[r], stt[r], sqjunk, xrb[r], True,
                                            [m_xld[r]], [m_xrb_free[r]])

                @blk.sync
                def _(sy, tt=tt, r=r, half=half, mk=m_ln_done[r], mkx=m_xrf_free):
                    W.wge(sy, mk, mkx)
                    W.inc(sy.dma_start(
                        bass.AP(xrf, 128 * (tt - half * OH), [[FC * TB, 128], [TB, FC], [1, 128]]),
                        xrb[r].ap().rearrange("t (j p) -> t j p", p=128),
                        transpose=True), "sH", 16)
                m_xrb_free[r] = W.snap()
            m_batchT = W.snap()

            for qc in range(QCOLS // 512):
                for i in range(4):
                    r = i % 2

                    @blk.gpsimd
                    def _(g, qc=qc, i=i, r=r, mkf=m_wstg_free[r]):
                        W.wge(g, mkf)
                        W.inc(g.dma_start(wstg[r][:, :],
                                          w_in[512 * qc + 128 * i: 512 * qc + 128 * (i + 1), :]),
                              "sL", 16)
                    m_wstg_ld[r] = W.snap()

                    @blk.sync
                    def _(sy, i=i, r=r, mk=m_wstg_ld[r], mkw=m_wqT_free):
                        W.wge(sy, mk, mkw)
                        W.inc(sy.dma_start(
                            bass.AP(wqT, 128 * i, [[FC * 512, 128], [512, FC], [1, 128]]),
                            wstg[r].ap().rearrange("t (j p) -> t j p", p=128),
                            transpose=True), "sH", 16)
                    m_wstg_free[r] = W.snap()
                m_wqTT = W.snap()

                for tt in tiles:
                    p2i = tt % 2
                    pb = pmm[p2i]

                    @blk.tensor
                    def _(t, tt=tt, pb=pb, half=half, mkb=m_batchT, mkw=m_wqTT,
                          mkf=m_mm_free[p2i]):
                        W.wge(t, mkb, mkw, mkf)
                        for fc in range(FC):
                            W.inc(t.matmul(pb[0:128, 0:512],
                                           bass.AP(xrf, fc * TB + 128 * (tt - half * OH),
                                                   [[FC * TB, 128], [1, 128]]),
                                           bass.AP(wqT, fc * 512, [[FC * 512, 128], [1, 512]]),
                                           start=(fc == 0), stop=(fc == FC - 1)), "sT", 1)
                    m_mm = W.snap()

                    qr = tt % 2
                    emit_rope(pb, qcos, qsin, tt, 512, qtm[qr],
                              [m_mm, m_qtm_free[qr]])
                    m_mm_free[p2i] = W.snap()
                    m_rope = W.snap()

                    @blk.sync
                    def _(sy, qr=qr, mk=m_rope, mkq=m_qstg_free[qr]):
                        W.wge(sy, mk, mkq)
                        W.inc(sy.dma_start(
                            qstg[qr].ap().rearrange("p (j t) -> p j t", t=128),
                            qtm[qr].ap().rearrange("t (j p) -> t j p", p=128),
                            transpose=True), "sH", 16)
                    m_qtm_free[qr] = W.snap()
                    m_qstgT = W.snap()

                    @blk.sync
                    def _(sy, qc=qc, tt=tt, qr=qr, mk=m_qstgT):
                        W.wge(sy, mk)
                        for hh in range(8):
                            h = 8 * qc + hh
                            g_ = h // HPG
                            hi = h % HPG
                            jb, srch = hh // 2, hh % 2
                            dst = bass.AP(Qp, (g_ // 2) * HPG * OWN + hi * OWN + 128 * tt
                                          + (64 * (g_ % 2)) * QPW, [[QPW, 64], [1, 128]])
                            src = bass.AP(qstg[qr], 128 * jb + (64 * srch) * qstg[qr].shape[1],
                                          [[qstg[qr].shape[1], 64], [1, 128]])
                            W.inc(sy.dma_start(dst, src), "sH", 16)
                    m_qstg_free[qr] = W.snap()
                m_wqT_free = W.snap()

        if cfg.debug:
            @blk.gpsimd
            def _(g):
                W.w(g)
                W.inc(g.dma_start(dbg_K2.ap(), K2.ap()), "sL", 16)
                W.inc(g.dma_start(dbg_Vt.ap(), Vt.ap()), "sL", 16)
                W.inc(g.dma_start(dbg_Qp.ap(), Qp.ap()), "sL", 16)

        p2.close()
        p12.close()
        p0.close()

        p3F = ExitStack()
        o_fm = self.sb(p3F, "o_fm", [128, CC * OWN], BF16, side="right")
        OFW = o_fm.shape[1]
        p3 = ExitStack()
        ejunk = self.sb(p3, "ejunk", [128, SEG], BF16, side="right")
        Pb2 = [self.sb(p3, f"Pb2_{i}", [128, 2 * SEG], BF16, side="right")
               for i in range(3)]
        PT = [self.sb(p3, f"PT{i}", [128, 2 * SEGB * HPG * QT], BF16, side="right")
              for i in range(3)]
        PTW = PT[0].shape[1]
        NSTR = 8
        strip1 = self.sb(p3, "strip1", [QT, 2 * HPG * NSTR], F32, side="right")
        strip2 = self.sb(p3, "strip2", [QT, 2 * HPG * NSTR], F32, side="right")
        bias_eo = self.sb(p3, "bias_eo", [QT, 2 * HPG], F32, side="right")
        rsum = self.sb(p3, "rsum", [QT, 2 * HPG], F32, side="right")
        rrec = self.sb(p3, "rrec", [QT, 2 * HPG], F32, side="right")
        rrow = self.sb(p3, "rrow", [1, 2 * HPG * QT], F32, side="right")
        bcast = self.sb(p3, "bcast", [128, HPG * QT], F32, side="right")
        o_stg = self.sb(p3, "o_stg", [128, HPG * QT], BF16, side="right")
        psS = [self.ps(p3, f"psS{i}", [128, 2 * SEG], F32) for i in range(2)]
        psOe = self.ps(p3, "psOe", [128, HPG * QT], F32)
        psOo = self.ps(p3, "psOo", [128, HPG * QT], F32)
        psR = self.ps(p3, "psR", [128, HPG * QT], F32)
        psB = self.ps(p3, "psB", [128, HPG * QT], F32)

        def slot_ap(sd, par):
            if sd[0] == "pair":
                return sd[1][0:QT, par * SEG:(par + 1) * SEG]
            return sd[1 + par][0:QT, 0:SEG]

        def scores_mms(t, gp, qcol, seg, sd, masked):
            for par in range(2):
                dst = slot_ap(sd, par)
                W.inc(t.matmul(dst,
                               Qp[64 * par:64 * par + 64, qcol:qcol + QT],
                               K2[64 * par:64 * par + 64,
                                  gp * S + SEG * seg: gp * S + SEG * (seg + 1)],
                               start=True, stop=True), "sT", 1)
                if masked:
                    W.inc(t.matmul(dst,
                                   identb[0:QT, 0:QT], tmaskb[0:QT, 0:SEG],
                                   start=False, stop=True,
                                   skip_group_check=True), "sT", 1)

        m_p3start = W.snap()
        m_psO_free = W.snap()
        m_v1 = W.snap()
        m_prev_tr = W.snap()
        m_prev_av = W.snap()
        m_s2_free = W.snap()

        for j in range(NSLOT):
            nseg = j + 1
            nI = 4 * nseg
            for gp in range(NPAIR):
                A0 = W.c["sA"]
                V0 = W.c["sV"]
                T0 = W.c["sT"]
                SL1 = [("pair", psS[0]), ("pair", psS[1]),
                       ("two", psOe, psOo), ("two", psB, psR)]
                D1 = len(SL1)
                scT = []
                redV = []
                tacc = T0
                vacc = V0
                for i in range(nI):
                    si = i % nseg
                    tacc += 4 if si == nseg - 1 else 2
                    scT.append(tacc)
                    vacc += 1 if SL1[i % D1][0] == "pair" else 2
                    redV.append(vacc)

                @blk.tensor
                def _(t, j=j, gp=gp, nseg=nseg, nI=nI, A0=A0, V0=V0, redV=redV,
                      SL1=SL1, D1=D1, mps=m_p3start):
                    W.wge(t, mps)
                    t.wait_ge(W.sem["sA"], A0)
                    t.wait_ge(W.sem["sV"], V0)
                    for i in range(nI):
                        hi, si = divmod(i, nseg)
                        if i >= D1:
                            t.wait_ge(W.sem["sV"], redV[i - D1])
                        scores_mms(t, gp, gp * HPG * OWN + hi * OWN + j * QT,
                                   si, SL1[i % D1], si == nseg - 1)

                @blk.vector
                def _(v, nseg=nseg, nI=nI, scT=scT, SL1=SL1, D1=D1, mv=m_v1):
                    W.wge(v, mv)
                    for i in range(nI):
                        hi, si = divmod(i, nseg)
                        v.wait_ge(W.sem["sT"], scT[i])
                        col = NSTR * 2 * hi + si
                        sd = SL1[i % D1]
                        if sd[0] == "pair":
                            sp = sd[1].ap()[0:QT, :].rearrange(
                                "q (two k) -> q two k", two=2)
                            dst = bass.AP(strip1, col,
                                          [[strip1.shape[1], QT], [NSTR, 2]])
                            W.inc(v.tensor_reduce(dst, sp, AX, Alu.max), "sV", 1)
                        else:
                            for par in range(2):
                                W.inc(v.tensor_reduce(
                                    strip1[0:QT, col + NSTR * par:
                                           col + NSTR * par + 1],
                                    sd[1 + par][0:QT, 0:SEG], AX, Alu.max),
                                    "sV", 1)
                m_p1 = W.snap()

                @blk.vector
                def _(v, nseg=nseg, mk=m_p1):
                    c = _Chained(W, v, "sV")
                    W.wge(v, mk)
                    for cc_ in range(2 * HPG):
                        c.tensor_reduce(rsum[0:QT, cc_:cc_ + 1],
                                        strip1[0:QT, NSTR * cc_:NSTR * cc_ + nseg],
                                        AX, Alu.max)
                    c.tensor_scalar(bias_eo[0:QT, :], rsum[0:QT, :], -0.125, None,
                                    Alu.mult)
                m_v1 = W.snap()
                m_bias = W.snap()

                if cfg.debug and j == 0 and gp == 0:
                    @blk.gpsimd
                    def _(g):
                        W.w(g)
                        W.inc(g.dma_start(dbg_s1.ap(), strip1.ap()), "sL", 16)
                        W.inc(g.dma_start(dbg_be.ap(), bias_eo.ap()), "sL", 16)

                A0 = W.c["sA"]
                T0 = W.c["sT"]
                H0 = W.c["sH"]
                V_bias = W.c["sV"]
                order = []
                for sseg in range(nseg):
                    order.append(("S", sseg))
                    if sseg >= 1:
                        order.append(("A", sseg - 1))
                order.append(("A", nseg - 1))
                scT2 = [0] * nI
                avT = [0] * nseg
                tacc = T0
                SL2 = [("pair", psS[0]), ("pair", psS[1]), ("two", psB, psR)]
                D2 = len(SL2)
                for kind, sseg in order:
                    if kind == "S":
                        for hi in range(4):
                            tacc += 4 if sseg == nseg - 1 else 2
                            scT2[4 * sseg + hi] = tacc
                    else:
                        tacc += 2 * SEGB
                        avT[sseg] = tacc
                expA = [A0 + 2 * (i + 1) for i in range(nI)]
                trH = [H0 + 16 * (i + 1) for i in range(nI)]

                @blk.tensor
                def _(t, j=j, gp=gp, nseg=nseg, A0=A0, V_bias=V_bias, expA=expA,
                      SL2=SL2, D2=D2, order=order, trH=trH, mo=m_psO_free):
                    W.wge(t, mo)
                    t.wait_ge(W.sem["sA"], A0)
                    t.wait_ge(W.sem["sV"], V_bias)
                    for kind, sseg in order:
                        if kind == "S":
                            for hi in range(4):
                                i = 4 * sseg + hi
                                if i >= D2:
                                    t.wait_ge(W.sem["sA"], expA[i - D2])
                                scores_mms(t, gp,
                                           gp * HPG * OWN + hi * OWN + j * QT,
                                           sseg, SL2[i % D2], sseg == nseg - 1)
                        else:
                            t.wait_ge(W.sem["sH"], trH[4 * sseg + 3])
                            for kb in range(SEGB):
                                first = (sseg == 0 and kb == 0)
                                last = (sseg == nseg - 1 and kb == SEGB - 1)
                                kti = sseg * SEGB + kb
                                W.inc(t.matmul(psOe[0:64, 0:HPG * QT],
                                               Vt[:, (2 * gp) * VST + kti * (DH + 1):
                                                  (2 * gp) * VST + kti * (DH + 1) + DH],
                                               PT[sseg % 3][:, kb * HPG * QT:
                                                            (kb + 1) * HPG * QT],
                                               start=first, stop=last,
                                               tile_position=(0, 0),
                                               skip_group_check=True), "sT", 1)
                                W.inc(t.matmul(psOo[64:128, 0:HPG * QT],
                                               Vt[:, (2 * gp + 1) * VST + kti * (DH + 1):
                                                  (2 * gp + 1) * VST + kti * (DH + 1) + DH],
                                               PT[sseg % 3][:, SEGB * HPG * QT + kb * HPG * QT:
                                                            SEGB * HPG * QT + (kb + 1) * HPG * QT],
                                               start=first, stop=last,
                                               tile_position=(0, 64),
                                               skip_group_check=True), "sT", 1)
                m_avdone = W.snap()

                @blk.scalar
                def _(sc, nseg=nseg, nI=nI, scT2=scT2, trH=trH, V_bias=V_bias,
                      SL2=SL2, D2=D2, mtr=m_prev_tr, ms2=m_s2_free):
                    W.wge(sc, mtr, ms2)
                    sc.wait_ge(W.sem["sV"], V_bias)
                    for i in range(nI):
                        sseg, hi = divmod(i, 4)
                        sc.wait_ge(W.sem["sT"], scT2[i])
                        if i >= 3:
                            sc.wait_ge(W.sem["sH"], trH[i - 3])
                        for par in range(2):
                            col = NSTR * (2 * hi + par) + sseg
                            W.inc(sc.activation(
                                Pb2[i % 3][0:QT, par * SEG:(par + 1) * SEG],
                                slot_ap(SL2[i % D2], par),
                                AF.Exp, scale=cAP(C_P125, QT),
                                bias=bias_eo[0:QT, 2 * hi + par:2 * hi + par + 1],
                                accum_out=strip2[0:QT, col:col + 1]), "sA", 1)
                m_p2 = W.snap()

                @blk.sync
                def _(sy, nseg=nseg, nI=nI, expA=expA, avT=avT, mpav=m_prev_av):
                    W.wge(sy, mpav)
                    for i in range(nI):
                        sseg = i // 4
                        sy.wait_ge(W.sem["sA"], expA[i])
                        if i % 4 == 0 and sseg >= 3:
                            sy.wait_ge(W.sem["sT"], avT[sseg - 3])
                        W.inc(sy.dma_start(
                            bass.AP(PT[sseg % 3], (i % 4) * QT,
                                    [[PTW, 128], [HPG * QT, 2 * SEGB], [1, QT]]),
                            Pb2[i % 3].ap()[0:QT, 0:2 * SEG]
                              .rearrange("t (j p) -> t j p", p=128),
                            transpose=True), "sH", 16)
                m_prev_tr = W.snap()
                m_prev_av = m_avdone

                @blk.vector
                def _(v, nseg=nseg, mk=m_p2):
                    c = _Chained(W, v, "sV")
                    W.wge(v, mk)
                    for cc_ in range(2 * HPG):
                        c.tensor_reduce(rsum[0:QT, cc_:cc_ + 1],
                                        strip2[0:QT, NSTR * cc_:NSTR * cc_ + nseg],
                                        AX, Alu.add)
                    c.reciprocal(rrec[0:QT, :], rsum[0:QT, :])
                m_s2_free = W.snap()
                m_rr = W.snap()

                m_rrow = None
                for rnd in range(2):
                    @blk.tensor
                    def _(t, rnd=rnd, mk=m_rr, mkc=m_rrow):
                        W.wge(t, mk, mkc)
                        for cc_ in range(HPG):
                            W.inc(t.matmul(psR[0:1, QT * cc_:QT * (cc_ + 1)],
                                           rrec[0:QT, HPG * rnd + cc_:HPG * rnd + cc_ + 1],
                                           ident_f[0:QT, 0:QT],
                                           start=True, stop=True), "sT", 1)
                    m_psr = W.snap()

                    @blk.vector
                    def _(v, rnd=rnd, mk=m_psr):
                        c = _Chained(W, v, "sV")
                        W.wge(v, mk)
                        c.tensor_copy(rrow[0:1, HPG * QT * rnd:HPG * QT * (rnd + 1)],
                                      psR[0:1, 0:HPG * QT])
                    m_rrow = W.snap()

                @blk.tensor
                def _(t, mk=m_rrow):
                    W.wge(t, mk)
                    for par in range(2):
                        W.inc(t.matmul(psB[64 * par:64 * par + 64, 0:HPG * QT],
                                       ones_f[0:1, 0:64],
                                       bass.AP(rrow, QT * par, [[rrow.shape[1], 1],
                                                               [2 * QT, HPG], [1, QT]]),
                                       start=True, stop=True,
                                       tile_position=(0, 64 * par)), "sT", 1)
                m_psb = W.snap()

                @blk.vector
                def _(v, mk=m_psb, mkav=m_avdone):
                    c = _Chained(W, v, "sV")
                    W.wge(v, mk, mkav)
                    c.tensor_copy(bcast[:, :], psB[0:128, :])
                    c.tensor_tensor(o_stg[0:64, :], psOe[0:64, :], bcast[0:64, :], Alu.mult)
                    c.tensor_tensor(o_stg[64:128, :], psOo[64:128, :],
                                    bcast[64:128, :], Alu.mult)
                m_psO_free = W.snap()
                m_ostg = W.snap()

                @blk.sync
                def _(sy, j=j, gp=gp, mk=m_ostg):
                    W.wge(sy, mk)
                    for par in range(2):
                        g_ = 2 * gp + par
                        for hi in range(HPG):
                            cch = (HPG * g_ + hi) // 2
                            rh = 64 * ((HPG * g_ + hi) % 2)
                            d = bass.AP(o_fm, cch * OWN + j * QT + rh * OFW, [[OFW, 64], [1, QT]])
                            s_ = bass.AP(o_stg, hi * QT + (64 * par) * o_stg.shape[1],
                                         [[o_stg.shape[1], 64], [1, QT]])
                            W.inc(sy.dma_start(d, s_), "sH", 16)

                if cfg.debug and j == 0 and gp == 0:
                    @blk.gpsimd
                    def _(g):
                        W.w(g)
                        W.inc(g.dma_start(dbg_Pb0.ap(), Pb2[0][:, 0:SEG]), "sL", 16)
                        W.inc(g.dma_start(dbg_s2.ap(), strip2.ap()), "sL", 16)
                        W.inc(g.dma_start(dbg_rr.ap(), rrec.ap()), "sL", 16)
                        W.inc(g.dma_start(dbg_ost.ap(), o_stg.ap()), "sL", 16)
                        W.inc(g.dma_start(dbg_bc.ap(), bcast.ap()), "sL", 16)

        self.serial = False
        if cfg.debug:
            @blk.gpsimd
            def _(g):
                W.w(g)
                W.inc(g.dma_start(dbg_ofm.ap(), o_fm.ap()), "sL", 16)

        p3.close()
        pQp.close()
        patt.close()
        p4 = ExitStack()
        woT = self.sb(p4, "woT", [128, CC * 512], BF16, side="right")
        wstg4 = [self.sb(p4, f"wstg4_{i}", [128, QCOLS], BF16, side="right") for i in range(2)]
        ysb = [self.sb(p4, f"ysb{i}", [128, 512], F32, side="right") for i in range(2)]
        psY = [self.ps(p4, f"psY{i}", [128, 512], F32) for i in range(2)]
        m_w4_free = [W.snap(), W.snap()]
        m_woT_free = W.snap()
        m_psY_free = [W.snap(), W.snap()]
        m_ysb_free = [W.snap(), W.snap()]

        for mb in range(D // 512):
            for i in range(4):
                r = i % 2

                @blk.gpsimd
                def _(g, mb=mb, i=i, r=r, mkf=m_w4_free[r]):
                    W.wge(g, mkf)
                    W.inc(g.dma_start(wstg4[r][:, :],
                                      w_out[512 * mb + 128 * i: 512 * mb + 128 * (i + 1), :]),
                          "sL", 16)
                m_ld = W.snap()

                @blk.sync
                def _(sy, i=i, r=r, mk=m_ld, mkw=m_woT_free):
                    W.wge(sy, mk, mkw)
                    W.inc(sy.dma_start(
                        bass.AP(woT, 128 * i, [[CC * 512, 128], [512, CC], [1, 128]]),
                        wstg4[r].ap().rearrange("t (j p) -> t j p", p=128),
                        transpose=True), "sH", 16)
                m_w4_free[r] = W.snap()
            m_woTT = W.snap()

            for tt in range(OT):
                p4i = tt % 2
                pb = psY[p4i]

                @blk.tensor
                def _(t, tt=tt, pb=pb, mk=m_woTT, mkf=m_psY_free[p4i]):
                    W.wge(t, mk, mkf)
                    for cc_ in range(CC):
                        W.inc(t.matmul(pb[0:128, 0:512],
                                       o_fm[:, cc_ * OWN + 128 * tt: cc_ * OWN + 128 * (tt + 1)],
                                       woT[:, cc_ * 512:(cc_ + 1) * 512],
                                       start=(cc_ == 0), stop=(cc_ == CC - 1)), "sT", 1)
                m_mm = W.snap()

                @blk.vector
                def _(v, pb=pb, p4i=p4i, mk=m_mm, mky=m_ysb_free[p4i]):
                    c = _Chained(W, v, "sV")
                    W.wge(v, mk, mky)
                    c.tensor_copy(ysb[p4i][:, :], pb[0:128, 0:512])
                m_psY_free[p4i] = W.snap()
                m_ev = W.snap()

                @blk.sync
                def _(sy, mb=mb, tt=tt, p4i=p4i, mk=m_ev):
                    W.wge(sy, mk)
                    W.inc(sy.dma_start(y_spill[128 * tt:128 * (tt + 1), 512 * mb:512 * (mb + 1)],
                                       ysb[p4i][:, :]), "sH", 16)
                m_ysb_free[p4i] = W.snap()
            m_woT_free = W.snap()

        p4.close()
        p3F.close()

        m_P5start = W.snap()
        p5 = ExitStack()
        p5X = ExitStack()
        xnf = self.sb(p5X, "xnf", [128, FC * OWN], BF16)
        gacc = self.sb(p5X, "gacc", [128, OT * D], F32)
        muI = self.sb(p5, "muI", [128, max(OT, 8)], F32, side="right")
        p5W = ExitStack()
        xt5 = self.sb(p5W, "xt5", [128, D], F32, side="right")
        st5 = self.sb(p5W, "st5", [128, 8], F32, side="right")
        qscrV = self.sb(p5W, "qscrV", [128, 64], F32, side="right")
        qsbV = self.sb(p5W, "qsbV", [128, 32], BF16, side="right")
        mch = self.sb(p5W, "mch", [128, max(D, 1024)], BF16, side="right")
        hring = self.sb(p5W, "hring", [128, 8 * OWN], BF16, side="right")
        wg_n = [self.sb(p5W, f"wg_n{i}", [128, D], BF16, side="right") for i in range(3)]
        wu_n = [self.sb(p5W, f"wu_n{i}", [128, D], BF16, side="right") for i in range(3)]
        wgT = [self.sb(p5W, f"wgT{i}", [128, FC * 128], BF16, side="right") for i in range(2)]
        wuT = [self.sb(p5W, f"wuT{i}", [128, FC * 128], BF16, side="right") for i in range(2)]
        THALF = min(512, OWN)
        NHALF = OWN // THALF
        silscr = [self.sb(p5W, f"silscr{i}", [128, THALF], F32, side="right")
                  for i in range(2)]
        sqh = self.sb(p5W, "sqh", [128, THALF], BF16, side="right")
        wp_n = [self.sb(p5W, f"wp_n{i}", [128, 1024], BF16, side="right") for i in range(2)]
        wpT = self.sb(p5W, "wpT", [128, 8 * D], BF16, side="right")
        psG = [self.ps(p5W, f"psG{i}", [128, THALF], F32) for i in range(2)]
        psU = [self.ps(p5W, f"psU{i}", [128, THALF], F32) for i in range(2)]
        psT5 = self.ps(p5W, "psT5", [128, 512], F32)
        psP = [self.ps(p5W, f"psP{i}", [128, 512], F32) for i in range(2)]
        NSUP = HF // 1024
        NDC = D // 512
        m_qB = {"B": None}
        xnb5 = hring[:, D:2 * D]

        def emit_quant(src_bf, dst_bf, Fw, wait0):
            ng = Fw // 64

            @blk.vector
            def _(v, mkb=m_qB["B"]):
                c = _Chained(W, v, "sV")
                W.wge(v, mkb, *wait0)
                c.tensor_reduce(qscrV[:, 0:ng],
                                src_bf.ap()[:, 0:Fw].rearrange("p (g k) -> p g k", k=64),
                                AX, Alu.add, apply_absolute_value=True)
                c.tensor_scalar(qsbV[:, 0:ng], qscrV[:, 0:ng], 1.0 / 64, 1e-8,
                                Alu.mult, Alu.max)
                c.tensor_scalar(qscrV[:, 32:32 + ng], qsbV[:, 0:ng], THR_BF16, None,
                                Alu.mult)
            m_s = W.snap()

            halves = ((blk.vector, "sV", 0), (blk.gpsimd, "sG", 1)) if POOL_QUANT \
                else ((blk.vector, "sV", 0),)
            nh = len(halves)
            Fh = Fw // nh
            ngh = ng // nh
            for eng_dec, semn, hx in halves:
                @eng_dec
                def _(v, semn=semn, hx=hx, mk=m_s):
                    c = _Chained(W, v, semn)
                    W.wge(v, mk)
                    c0 = hx * Fh
                    g0 = hx * ngh
                    srcr = src_bf.ap()[:, c0:c0 + Fh].rearrange("p (g k) -> p g k", k=64)
                    dstr = dst_bf.ap()[:, c0:c0 + Fh].rearrange("p (g k) -> p g k", k=64)
                    mr = mch.ap()[:, c0:c0 + Fh].rearrange("p (g k) -> p g k", k=64)
                    thr_b = qscrV[:, 32 + g0:32 + g0 + ngh].unsqueeze(2) \
                        .broadcast_to([128, ngh, 64])
                    s_b = qsbV[:, g0:g0 + ngh].unsqueeze(2).broadcast_to([128, ngh, 64])
                    c.scalar_tensor_tensor(mr, srcr, cAP(C_NEG1), thr_b,
                                           Alu.mult, Alu.is_gt)
                    c.tensor_tensor(dstr, srcr, thr_b, Alu.is_gt)
                    c.tensor_tensor(dstr, dstr, mr, Alu.subtract)
                    c.tensor_tensor(dstr, dstr, s_b, Alu.mult)
            m_qB["B"] = W.snap()
            return m_qB["B"]

        m_ln5_done = W.snap()
        m_xnb5_free = W.snap()
        for tt in range(OT):
            @blk.sync
            def _(sy, tt=tt, mkf=m_ln5_done):
                W.wge(sy, mkf)
                W.inc(sy.dma_start(xt5[:, :], x_own[128 * tt:128 * (tt + 1), :]),
                      "sH", 16)
            m_ld5 = W.snap()

            m_ln5_done = emit_ln_tile(xt5, st5, hring[:, 0:D], xnb5, False,
                                      [m_ld5], [m_xnb5_free], save_tt=tt)

            @blk.sync
            def _(sy, tt=tt, mk=m_ln5_done):
                W.wge(sy, mk)
                W.inc(sy.dma_start(
                    bass.AP(xnf, 128 * tt, [[FC * OWN, 128], [OWN, FC], [1, 128]]),
                    xnb5.rearrange("t (j p) -> t j p", p=128),
                    transpose=True), "sH", 16)
            m_xnb5_free = W.snap()
        m_xnf = W.snap()

        if cfg.debug:
            @blk.gpsimd
            def _(g):
                W.w(g)
                W.inc(g.dma_start(dbg_xnf.ap(), xnf.ap()), "sL", 16)

        m_w_free = [m_P5start, m_P5start, m_P5start]
        m_wT_free = [m_P5start, m_P5start]
        m_gu_free = [m_P5start, m_P5start]
        m_proj_done = m_P5start
        m_sqh_free = m_P5start
        m_wp_free = [m_P5start, m_P5start]
        m_wpT_free = m_P5start
        m_gacc_free = [m_P5start, m_P5start]

        HCH = cfg.HCH
        m_wTT_of = {}
        m_h_super = {}

        def stage_L(k):
            if not (0 <= k < HCH):
                return
            r3 = k % 3

            @blk.gpsimd
            def _(g, k=k, r3=r3, mkf=m_w_free[r3]):
                W.wge(g, mkf)
                W.inc(g.dma_start(wg_n[r3][:, :], w_gu[128 * k:128 * (k + 1), :]),
                      "sL", 16)
                W.inc(g.dma_start(wu_n[r3][:, :],
                                  w_gu[HF + 128 * k: HF + 128 * (k + 1), :]), "sL", 16)
            m_ld_of[k] = W.snap()

        def stage_Q(k):
            if not (0 <= k < HCH):
                return
            r3 = k % 3
            emit_quant(wg_n[r3], wg_n[r3], D, [m_ld_of[k]])
            m_q_of[k] = emit_quant(wu_n[r3], wu_n[r3], D, [m_ld_of[k]])

        def stage_T(k):
            if not (0 <= k < HCH):
                return
            r3 = k % 3
            r2 = k % 2

            @blk.sync
            def _(sy, r3=r3, r2=r2, mk=m_q_of[k], mkT=m_wT_free[r2]):
                W.wge(sy, mk, mkT)
                W.inc(sy.dma_start(
                    bass.AP(wgT[r2], 0, [[FC * 128, 128], [128, FC], [1, 128]]),
                    wg_n[r3].ap().rearrange("t (j p) -> t j p", p=128), transpose=True),
                      "sH", 16)
                W.inc(sy.dma_start(
                    bass.AP(wuT[r2], 0, [[FC * 128, 128], [128, FC], [1, 128]]),
                    wu_n[r3].ap().rearrange("t (j p) -> t j p", p=128), transpose=True),
                      "sH", 16)
            m_wTT_of[k] = W.snap()
            m_w_free[r3] = m_wTT_of[k]

        def stage_Ma(k):
            b = k % 8
            r2 = k % 2
            for hf in range(NHALF):
                gi = (2 * b + hf) % 2

                @blk.tensor
                def _(t, hf=hf, r2=r2, gi=gi, mk=m_wTT_of[k], mkf=m_gu_free[gi]):
                    W.wge(t, mk, mkf)
                    for fc in range(FC):
                        W.inc(t.matmul(psG[gi][0:128, 0:THALF],
                                       wgT[r2][:, fc * 128:(fc + 1) * 128],
                                       bass.AP(xnf, fc * OWN + THALF * hf,
                                               [[FC * OWN, 128], [1, THALF]]),
                                       start=(fc == 0), stop=(fc == FC - 1)), "sT", 1)
                    for fc in range(FC):
                        W.inc(t.matmul(psU[gi][0:128, 0:THALF],
                                       wuT[r2][:, fc * 128:(fc + 1) * 128],
                                       bass.AP(xnf, fc * OWN + THALF * hf,
                                               [[FC * OWN, 128], [1, THALF]]),
                                       start=(fc == 0), stop=(fc == FC - 1)), "sT", 1)
                m_mm = W.snap()
                m_wT_free[r2] = m_mm

                @blk.scalar
                def _(sc, gi=gi, mk=m_mm):
                    c = _Chained(W, sc, "sA")
                    W.wge(sc, mk)
                    c.activation(silscr[gi][:, :], psG[gi][0:128, 0:THALF], AF.Silu)
                m_sil_of[(k, hf)] = W.snap()

        def stage_Mb(k):
            nonlocal m_sqh_free
            b = k % 8
            for hf in range(NHALF):
                gi = (2 * b + hf) % 2

                @blk.vector
                def _(v, b=b, hf=hf, gi=gi, mk=m_sil_of[(k, hf)],
                      mkp=m_proj_done, mks=m_sqh_free):
                    c = _Chained(W, v, "sV")
                    W.wge(v, mk, mkp, mks)
                    hs = hring[:, b * OWN + THALF * hf: b * OWN + THALF * (hf + 1)]
                    c.tensor_tensor(hs, silscr[gi][:, :], psU[gi][0:128, 0:THALF],
                                    Alu.mult)
                    c.tensor_tensor(sqh[:, :], hs, hs, Alu.mult)
                m_gu_free[gi] = W.snap()
                m_h = W.snap()
                m_h_super[k // 8] = m_h

                @blk.tensor
                def _(t, k=k, hf=hf, mk=m_h):
                    W.wge(t, mk)
                    for ttl in range(THALF // 128):
                        tt = (THALF // 128) * hf + ttl
                        W.inc(t.matmul(psT5[0:128, tt:tt + 1],
                                       sqh[:, 128 * ttl:128 * (ttl + 1)],
                                       ones_b[:, 0:1],
                                       start=(k == 0 and hf == 0 and ttl == 0),
                                       stop=(k == HCH - 1 and hf == NHALF - 1
                                             and ttl == THALF // 128 - 1),
                                       skip_group_check=True), "sT", 1)
                m_sqh_free = W.snap()

        m_wpTr = {}

        def emit_wp(ks, i):
            r = i % 2

            @blk.gpsimd
            def _(g, ks=ks, i=i, r=r, mkf=m_wp_free[r]):
                W.wge(g, mkf)
                W.inc(g.dma_start(wp_n[r][:, :],
                                  w_proj[128 * i:128 * (i + 1),
                                         1024 * ks:1024 * (ks + 1)]), "sL", 16)
            m_ld = W.snap()

            m_q = emit_quant(wp_n[r], wp_n[r], 1024, [m_ld])

            @blk.sync
            def _(sy, i=i, r=r, mk=m_q, mkT=m_wpT_free):
                W.wge(sy, mk, mkT)
                W.inc(sy.dma_start(
                    bass.AP(wpT, 128 * i, [[8 * D, 128], [D, 8], [1, 128]]),
                    wp_n[r].ap()[:, 0:1024].rearrange("t (j p) -> t j p", p=128),
                    transpose=True), "sH", 16)
            m_wp_free[r] = W.snap()
            m_wpTr[(ks, i)] = m_wp_free[r]

        def super_tail(ks):
            nonlocal m_proj_done, m_wpT_free
            for dc in range(NDC):
                for tt in range(OT):
                    pi = (dc * OT + tt) % 2
                    pb = psP[pi]

                    @blk.tensor
                    def _(t, tt=tt, dc=dc, pb=pb, mk=m_wpTr[(ks, 4 * dc + 3)],
                          mkh=m_h_super[ks], mkf=m_gacc_free[pi]):
                        W.wge(t, mk, mkh, mkf)
                        for b in range(8):
                            W.inc(t.matmul(pb[0:128, 0:512],
                                           hring[:, b * OWN + 128 * tt: b * OWN + 128 * (tt + 1)],
                                           wpT[:, b * D + 512 * dc: b * D + 512 * (dc + 1)],
                                           start=(b == 0), stop=(b == 7)), "sT", 1)
                    m_mm = W.snap()

                    @blk.vector
                    def _(v, ks=ks, tt=tt, dc=dc, pb=pb, mk=m_mm):
                        c = _Chained(W, v, "sV")
                        W.wge(v, mk)
                        dst = gacc[:, tt * D + 512 * dc: tt * D + 512 * (dc + 1)]
                        if ks == 0:
                            c.tensor_copy(dst, pb[0:128, 0:512])
                        else:
                            c.tensor_tensor(dst, dst, pb[0:128, 0:512], Alu.add)
                    m_gacc_free[pi] = W.snap()
            m_proj_done = W.snap()
            m_wpT_free = m_proj_done

        m_ld_of = {}
        m_q_of = {}
        m_sil_of = {}
        stage_L(0)
        stage_L(1)
        stage_Q(0)
        stage_T(0)
        NWP = D // 128
        wpcum = [min(NWP, (NWP * x + 5) // 6) for x in range(7)]
        for k in range(HCH + 1):
            stage_L(k + 2)
            stage_Q(k + 1)
            stage_T(k + 1)
            if k >= 1:
                stage_Mb(k - 1)
                if (k - 1) % 8 == 7:
                    super_tail((k - 1) // 8)
            c = k % 8
            if k < HCH and c >= 2:
                for i in range(wpcum[c - 2], wpcum[c - 1]):
                    emit_wp(k // 8, i)
            if k < HCH:
                stage_Ma(k)


        @blk.scalar
        def _(sc):
            c = _Chained(W, sc, "sA")
            W.w(sc)
            c.activation(muI[:, 0:OT], psT5[0:128, 0:OT], AF.Sqrt,
                         scale=cAP(C_INVHF), bias=cAP(C_EPSF32))

        @blk.vector
        def _(v):
            c = _Chained(W, v, "sV")
            W.w(v)
            c.reciprocal(muI[:, 0:OT], muI[:, 0:OT])
        m_muI = W.snap()

        if cfg.debug:
            @blk.vector
            def _(v):
                c = _Chained(W, v, "sV")
                W.w(v)
                c.tensor_copy(xt5[:, 0:OT], psT5[0:128, 0:OT])

            @blk.gpsimd
            def _(g):
                W.w(g)
                W.inc(g.dma_start(dbg_gacc.ap(), gacc.ap()), "sL", 16)
                W.inc(g.dma_start(dbg_muI.ap()[:, 0:OT], muI[:, 0:OT]), "sL", 16)
                W.inc(g.dma_start(dbg_psT.ap()[:, 0:OT], xt5[:, 0:OT]), "sL", 16)
                W.inc(g.dma_start(dbg_sqh.ap(), sqh[:, :]), "sL", 16)

        m_P6start = W.snap()
        p5W.close()
        p6 = ExitStack()
        xt6 = [self.sb(p6, f"xt6_{i}", [128, D], F32, side="right") for i in range(2)]
        yh6 = [self.sb(p6, f"yh6_{i}", [128, D], F32, side="right") for i in range(2)]
        gsil = [self.sb(p6, f"gsil{i}", [128, D], F32, side="right") for i in range(2)]
        xnh = self.sb(p6, "xnh", [128, D], F32, side="right")
        outt = [self.sb(p6, f"outt{i}", [128, D], F32, side="right") for i in range(2)]
        m_cmb_done = [W.snap(), W.snap()]
        m_out_free = [W.snap(), W.snap()]

        for tt in range(OT):
            r = tt % 2

            @blk.sync
            def _(sy, tt=tt, r=r, mkf=m_cmb_done[r]):
                W.wge(sy, mkf)
                W.inc(sy.dma_start(xt6[r][:, :], x_own[128 * tt:128 * (tt + 1), :]),
                      "sH", 16)
                W.inc(sy.dma_start(yh6[r][:, :], y_spill[128 * tt:128 * (tt + 1), :]),
                      "sH", 16)
            m_ld = W.snap()

            @blk.scalar
            def _(sc, tt=tt, r=r, mk=m_muI, mkf=m_cmb_done[r]):
                c = _Chained(W, sc, "sA")
                W.wge(sc, mk, mkf)
                c.activation(gsil[r][:, :], gacc[:, tt * D:(tt + 1) * D], AF.Silu,
                             scale=muI[:, tt:tt + 1])
            m_gs = W.snap()

            @blk.vector
            def _(v, tt=tt, r=r, mk=m_ld, mkg=m_gs, mko=m_out_free[r]):
                c = _Chained(W, v, "sV")
                W.wge(v, mk, mkg, mko)
                c.tensor_scalar(xnh[:, :], xt6[r][:, :],
                                muR[:, 2 * tt:2 * tt + 1], muR[:, 2 * tt + 1:2 * tt + 2],
                                Alu.subtract, Alu.mult)
                c.tensor_tensor(xnh[:, :], xnh[:, :], yh6[r][:, :], Alu.add)
                c.tensor_tensor(xnh[:, :], xnh[:, :], gsil[r][:, :], Alu.mult)
                c.tensor_tensor(outt[r][:, :], xnh[:, :], xt6[r][:, :], Alu.add)
            m_cmb_done[r] = W.snap()
            m_cmb = W.snap()

            @blk.sync
            def _(sy, tt=tt, r=r, mk=m_cmb):
                W.wge(sy, mk)
                W.inc(sy.dma_start(out_own[128 * tt:128 * (tt + 1), :], outt[r][:, :]),
                      "sH", 16)
            m_out_free[r] = W.snap()

        @blk.sync
        def _(sy):
            W.w(sy)

        phase_barrier()
        p6.close()
        p5.close()
        p5X.close()
        self.es.close()
        return self.nc


def _tables(S):
    invf = 1.0 / (10000.0 ** (np.arange(0, 64, 2, dtype=np.float64) / 64.0))
    ang = np.arange(S, dtype=np.float64)[:, None] * invf[None, :]
    return np.cos(ang).astype(np.float32), np.sin(ang).astype(np.float32)


def make_inputs_for_core(cfg: Cfg, core, x, w_in, w_out, rms_w, w_gu, w_proj):
    import ml_dtypes
    S, D, QT, SEG = cfg.S, cfg.D, cfg.QT, cfg.SEG
    seq = core // 4
    cp = core % 4
    own_idx = np.concatenate([SEG * j + QT * cp + np.arange(QT) for j in range(cfg.NSLOT)])
    kc, ks_ = _tables(S)
    ql = np.arange(QT)[:, None]
    kk = np.arange(SEG)[None, :]
    tri = np.where(kk <= QT * cp + ql, 0.0, MASK_NEG).astype(np.float32)
    consts = np.zeros((128, NCONST), np.float32)
    consts[:, C_INV64] = 1.0 / 64
    consts[:, C_P125] = 0.125
    consts[:, C_EPS_LN] = 1e-5
    consts[:, C_EPS_RMS] = 1e-6
    consts[:, C_INVHF] = 1.0 / cfg.HF
    consts[:, C_EPSF32] = 1.1920929e-7
    consts[:, C_NEG1] = -1.0
    return {
        "x_seq": np.ascontiguousarray(x[seq]),
        "x_own": np.ascontiguousarray(x[seq][own_idx]),
        "w_in": w_in, "w_out": w_out, "w_gu": w_gu, "w_proj": w_proj,
        "rms_rep": np.ascontiguousarray(np.broadcast_to(rms_w[None, :], (128, D))).astype(np.float32),
        "kcos": kc, "ksin": ks_,
        "qcos": np.ascontiguousarray(kc[own_idx]), "qsin": np.ascontiguousarray(ks_[own_idx]),
        "trimaskb": tri.astype(ml_dtypes.bfloat16),
        "identb": np.eye(128, dtype=np.float32).astype(ml_dtypes.bfloat16),
        "consts": consts,
        "ones_in": np.ones((128, 64), np.float32),
        "ident_in": np.eye(128, dtype=np.float32),
        "onesb_in": np.ones((128, 1), np.float32).astype(ml_dtypes.bfloat16),
    }, own_idx, seq


def kernel(L0_attn_x, L0_attn_w_in, L0_attn_w_out, L0_attn_rms_w,
           L0_ffn_w_gate_up, L0_ffn_w_proj):
    cfg = Cfg()
    x = np.asarray(L0_attn_x)
    nc = B(cfg).build()
    in_maps = []
    metas = []
    for core in range(8):
        m, own_idx, seq = make_inputs_for_core(
            cfg, core, x, np.asarray(L0_attn_w_in), np.asarray(L0_attn_w_out),
            np.asarray(L0_attn_rms_w), np.asarray(L0_ffn_w_gate_up),
            np.asarray(L0_ffn_w_proj))
        in_maps.append(m)
        metas.append((own_idx, seq))
    res = run_bass_kernel_spmd(nc, in_maps, core_ids=list(range(8)))
    out = np.empty_like(x)
    for core in range(8):
        own_idx, seq = metas[core]
        out[seq][own_idx] = np.asarray(res.results[core]["out_own"])
    return out
